# version 11
# baseline (speedup 1.0000x reference)
import numpy as np

import bass_rust
import concourse.bass as bass
import concourse.mybir as mybir
from concourse.bass_utils import run_bass_kernel_spmd

F32 = mybir.dt.float32
BF16 = mybir.dt.bfloat16
AF = mybir.ActivationFunctionType
ALU = mybir.AluOpType
AX = mybir.AxisListType

S = 8192
D = 2048
DH = 128
NC = 8
HPC = 2
TS = S // NC
QT = 512
NQT = S // QT
SCALE = float(DH) ** -0.5
EPS_LN = 1e-5

DEBUG_PHASE = 0


class Plan:
    ENGS = ("tensor", "vector", "scalar", "gpsimd", "sync")

    def __init__(self):
        self.ops = {e: [] for e in self.ENGS}
        self.cnt = {e: 0 for e in self.ENGS}
        self.dma = {}
        self.cc = 0
        self.sems = {}

    def wait(self, eng, sem_name, value):
        if value is None:
            return
        if isinstance(value, list):
            self.ops[eng].append(
                lambda e, s=sem_name, v=value: e.wait_ge(self.sems[s], v[0])
                if v[0] > 0
                else None
            )
            return
        if value <= 0:
            return
        self.ops[eng].append(lambda e, s=sem_name, v=value: e.wait_ge(self.sems[s], v))

    def op(self, eng, fn, inc=False):
        if inc:
            self.cnt[eng] += 1
            v = self.cnt[eng]
            self.ops[eng].append(
                lambda e, f=fn, s=eng: f(e).then_inc(self.sems[s], 1)
            )
            return v
        self.ops[eng].append(fn)
        return None

    def dma_op(self, queue, fn, stream):
        self.dma.setdefault(stream, 0)
        self.dma[stream] += 16
        self.ops[queue].append(
            lambda e, f=fn, n=stream: f(e).then_inc(self.sems[n], 16)
        )
        return self.dma[stream]

    def cc_op(self, fn):
        self.cc += 1
        self.ops["gpsimd"].append(lambda e, f=fn: f(e).then_inc(self.sems["cc"], 1))
        return self.cc

    def emit(self, nc):
        import contextlib

        with contextlib.ExitStack() as st:
            for e in self.ENGS:
                self.sems[e] = st.enter_context(nc.semaphore("sm_" + e))
            for q in self.dma:
                self.sems[q] = st.enter_context(nc.semaphore("sm_" + q))
            self.sems["cc"] = st.enter_context(nc.semaphore("sm_cc"))
            with nc.Block() as block:
                for name, deco in (
                    ("tensor", block.tensor),
                    ("vector", block.vector),
                    ("scalar", block.scalar),
                    ("gpsimd", block.gpsimd),
                    ("sync", block.sync),
                ):
                    ops = self.ops[name]

                    def body(e, ops=ops):
                        for f in ops:
                            f(e)

                    deco(body)


def build_program(debug_phase=None):
    if debug_phase is None:
        debug_phase = DEBUG_PHASE
    nc = bass.Bass("TRN2", target_bir_lowering=False)
    p = Plan()

    dt = nc.dram_tensor
    x_tm = dt("x_tm", [S, D], F32, kind="ExternalInput").ap()
    x_own = dt("x_own", [TS, D], F32, kind="ExternalInput").ap()
    xT_own = dt("xT_own", [D, TS], F32, kind="ExternalInput").ap()
    wq_c = dt("wq_c", [128, 16, 256], F32, kind="ExternalInput").ap()
    wk_c = dt("wk_c", [128, 16, 256], F32, kind="ExternalInput").ap()
    wv_c = dt("wv_c", [128, 16, 256], F32, kind="ExternalInput").ap()
    wo_c = dt("wo_c", [128, 32, 128], F32, kind="ExternalInput").ap()
    w1T_b = dt("w1T_b", [64, 128, 2048], F32, kind="ExternalInput").ap()
    w2T_b = dt("w2T_b", [16, 8, 128, 1024], F32, kind="ExternalInput").ap()
    w1_sl = dt("w1_sl", [8, 128, 2048], F32, kind="ExternalInput").ap()
    w2_sl = dt("w2_sl", [16, 128, 1024], F32, kind="ExternalInput").ap()
    b1_r = dt("b1_r", [128, 64], F32, kind="ExternalInput").ap()
    b2_r = dt("b2_r", [128, 16], F32, kind="ExternalInput").ap()
    ident_h = dt("ident_h", [128, 128], BF16, kind="ExternalInput").ap()
    masks_h = dt("masks_h", [128, 4 * QT], BF16, kind="ExternalInput").ap()

    yT = dt("yT", [D, TS], F32, kind="ExternalOutput").ap()

    w1q_d = dt("w1q_d", [64, 128, 2048], BF16, kind="Internal").ap()
    w2q_d = dt("w2q_d", [16, 8, 128, 1024], BF16, kind="Internal").ap()
    w1q_m = dt("w1q_m", [8, 128, 2048], BF16, kind="Internal").ap()
    w2q_m = dt("w2q_m", [2, 8, 128, 1024], BF16, kind="Internal").ap()
    xn_m = dt("xn_m", [TS, D], BF16, kind="Internal").ap()
    xn_ag = dt("xn_ag", [S, D], BF16, kind="Internal").ap()
    s_part = dt("s_part", [NC, D, TS], F32, kind="Internal").ap()
    s_red = dt("s_red", [D, TS], F32, kind="Internal").ap()
    sc_in = dt("sc_in", [1, 16], F32, kind="Internal").ap()
    sc_out = dt("sc_out", [1, 16], F32, kind="Internal").ap()

    dbg = {}
    if debug_phase >= 1:
        dbg["xn_cT"] = dt("dbg_xn_cT", [128, 16 * TS], BF16, kind="ExternalOutput").ap()
        dbg["QT"] = dt("dbg_QT", [128, HPC * S], BF16, kind="ExternalOutput").ap()
        dbg["KT"] = dt("dbg_KT", [128, HPC * S], BF16, kind="ExternalOutput").ap()
        dbg["Vtok"] = dt("dbg_Vtok", [128, (S // 128) * 256], BF16, kind="ExternalOutput").ap()
    if debug_phase >= 2:
        dbg["spart"] = dt("dbg_spart", [NC * D, TS], F32, kind="ExternalOutput").ap()
        dbg["theta"] = dt("dbg_theta", [128, 4], F32, kind="ExternalOutput").ap()
        dbg["biasm"] = dt("dbg_biasm", [128, 32], F32, kind="ExternalOutput").ap()
    if debug_phase >= 3:
        dbg["g"] = dt("dbg_g", [128, 16 * TS], BF16, kind="ExternalOutput").ap()
    if debug_phase >= 4:
        dbg["h0"] = dt("dbg_h0", [128, 64 * 512], BF16, kind="ExternalOutput").ap()
        dbg["y0"] = dt("dbg_y0", [128, 16 * 512], F32, kind="ExternalOutput").ap()
        dbg["w1q"] = dt("dbg_w1q", [8, 128, 2048], BF16, kind="ExternalOutput").ap()
        dbg["w2q"] = dt("dbg_w2q", [16, 8, 128, 1024], BF16, kind="ExternalOutput").ap()

    K = 1024
    ARENA = 206 * K
    arena = nc.alloc_sbuf_tensor("arena", [128, ARENA], mybir.dt.uint8)
    BASE = None
    for a in nc.m.functions[0].allocations:
        if (
            isinstance(a, mybir.MemoryLocationSet)
            and a.memorylocations
            and a.memorylocations[0].name == "arena"
        ):
            BASE = a.memorylocations[0].addr
    assert BASE is not None and BASE + ARENA <= 229376, BASE
    alloc = nc.alloc_sbuf_tensor_at

    def sb(name, shape, dtyp, at):
        return alloc(name, shape, dtyp, offset=BASE + at)

    xn_c = sb("xn_c", [128, 16 * TS], BF16, 0)
    ident = sb("identsb", [128, 128], BF16, 32 * K)
    ones128 = sb("ones128", [128, 128], BF16, 32 * K + 256)
    b1s = sb("b1s", [128, 64], F32, 32 * K + 512)
    b2s = sb("b2s", [128, 16], F32, 32 * K + 768)
    biasm = sb("biasm", [128, 32], F32, 32 * K + 832)
    qmax = sb("qmax", [128, 32], F32, 32 * K + 960)
    kmax = sb("kmax", [128, 2], F32, 33 * K + 64)
    theta = sb("theta", [128, 4], F32, 33 * K + 96)
    scpart = sb("scpart", [128, 16], F32, 33 * K + 128)
    small = sb("small", [128, 32], F32, 33 * K + 192)
    bst = sb("bst", [128, 48], F32, 33 * K + 320)
    ones128f = sb("ones128f", [128, 128], F32, 33 * K + 512)
    wo_sb = sb("wo_sb", [128, 2 * 16 * 128], BF16, 34 * K)
    wq_sb = sb("wq_sb", [128, 16 * 256], BF16, 42 * K)
    wk_sb = sb("wk_sb", [128, 16 * 256], BF16, 50 * K)
    wv_sb = sb("wv_sb", [128, 16 * 256], BF16, 58 * K)

    A0 = 66 * K
    wstg = sb("wstg", [128, 4 * 4096], F32, A0)
    QTs = sb("QTs", [128, HPC * S], BF16, A0)
    KTs = sb("KTs", [128, HPC * S], BF16, A0 + 32 * K)
    Vtok = sb("Vtok", [128, (S // 128) * 256], BF16, A0 + 64 * K)
    h_sb = sb("h_sb", [128, 64 * 512], BF16, A0)
    y_sb = sb("y_sb", [128, 16 * 512], F32, A0 + 64 * K)

    B0 = A0 + 96 * K
    x_t = sb("x_t", [128, 2 * D], F32, B0)
    xn_tm = sb("xn_tm", [128, 2 * D], BF16, B0 + 16 * K)
    xn_str = sb("xn_str", [128, 16 * QT], BF16, B0 + 24 * K)
    scstg = sb("scstg", [128, 2 * D], F32, B0)
    qstg = sb("qstg", [128, 2 * D], F32, B0)
    qout = sb("qout", [128, D], BF16, B0 + 40 * K)
    pT = sb("pT", [128, 3 * QT], BF16, B0 + 24 * K)
    obar = sb("obar", [128, 2 * QT], BF16, B0 + 27 * K)
    rinv = sb("rinv", [128, QT], F32, B0 + 29 * K)
    sprj = sb("sprj", [128, 2 * QT], F32, B0 + 31 * K)
    masks = sb("maskssb", [128, 4 * QT], BF16, B0 + 35 * K)
    sq2 = sb("sq2", [128, 2 * QT], BF16, B0 + 39 * K)
    g_sb = sb("g_sb", [128, 16 * TS], BF16, B0)
    w1q_sb = sb("w1q_sb", [128, 2 * 2048], BF16, B0 + 32 * K)
    w2q_sb = sb("w2q_sb", [128, 2 * 1024], BF16, B0 + 40 * K)
    s_sb = sb("s_sb", [128, 2 * 512], F32, B0 + 32 * K)
    xr_sb = sb("xr_sb", [128, 2 * 512], F32, B0 + 36 * K)
    o_sb = sb("o_sb", [128, 2 * 512], F32, B0 + 40 * K)
    assert B0 + 44 * K <= ARENA, (B0 + 44 * K, ARENA)

    pal = nc.alloc_psum_tensor
    pA = pal("pA", [128, 1024], BF16)
    pB = pal("pB", [128, 1024], BF16)
    pC = pal("pC", [128, 512], F32)
    pD = pal("pD", [128, 512], F32)
    pE = pal("pE", [128, 512], F32)
    pF = pal("pF", [128, 512], F32)
    pG = pal("pG", [128, 512], F32)
    pH = pal("pH", [128, 512], F32)
    psum_free = {id(ps): (None, 0) for ps in (pC, pD, pE, pF, pG, pH)}
    ptr_free = [(None, 0), (None, 0)]

    def gate_psum(ps):
        eng, cnt = psum_free[id(ps)]
        if eng is not None:
            p.wait("tensor", eng, cnt)

    p.op("gpsimd", lambda e: e.memset(ones128[:, :], 1.0))
    p.op("gpsimd", lambda e: e.memset(ones128f[:, :], 1.0))
    ge_ones = p.op("gpsimd", lambda e: e.memset(scpart[:, :], 0.0), inc=True)

    p.dma_op("sync", lambda e: e.dma_start(out=ident[:, :], in_=ident_h), "ini")
    p.dma_op("sync", lambda e: e.dma_start(out=b1s[:, :], in_=b1_r), "ini")
    p.dma_op("sync", lambda e: e.dma_start(out=b2s[:, :], in_=b2_r), "ini")
    for idx, src in enumerate((wq_c, wk_c, wv_c)):
        p.dma_op(
            "sync",
            lambda e, idx=idx, src=src: e.dma_start(
                out=wstg[:, idx * 4096 : (idx + 1) * 4096],
                in_=src.rearrange("p c f -> p (c f)"),
            ),
            "ini",
        )
    d_ini = p.dma_op(
        "sync",
        lambda e: e.dma_start(
            out=wstg[:, 3 * 4096 : 4 * 4096],
            in_=wo_c.rearrange("p c f -> p (c f)"),
        ),
        "ini",
    )
    p.op("scalar", lambda e: e.preload_act_table(AF.Sqrt))
    p.wait("vector", "ini", d_ini)
    for idx, wdst in enumerate((wq_sb, wk_sb, wv_sb, wo_sb)):
        cast_done = p.op(
            "vector",
            lambda e, idx=idx, wdst=wdst: e.tensor_copy(
                wdst[:, :], wstg[:, idx * 4096 : (idx + 1) * 4096]
            ),
            inc=True,
        )

    xt_free = [0, 0]
    small_free = [0, 0]
    ln_cnt = [0]

    def emit_ln_tile(src_ap, dst_slot, xt_gate=None, extra_wait=None):
        j = ln_cnt[0]
        ln_cnt[0] += 1
        par = j % 2
        sl = x_t[:, par * D : (par + 1) * D]
        p.wait("gpsimd", "scalar", xt_free[par])
        dv = p.dma_op(
            "gpsimd",
            lambda e, src_ap=src_ap, sl=sl: e.dma_start(out=sl, in_=src_ap),
            "xt%d" % par,
        )
        p.wait("vector", "xt%d" % par, dv)
        p.wait("vector", "scalar", small_free[par])
        so = 12 * par
        bo = 24 * par
        for i in range(4):
            p.op(
                "vector",
                lambda e, i=i, sl=sl, bo=bo: e.bn_stats(
                    bst[:, bo + 6 * i : bo + 6 * i + 6],
                    sl[:, 512 * i : 512 * i + 512],
                ),
            )
        p.op("vector", lambda e: e.drain())
        p.op(
            "vector",
            lambda e, so=so, bo=bo: e.bn_aggr(
                small[:, so : so + 2], bst[:, bo : bo + 24]
            ),
        )
        p.op("vector", lambda e: e.drain())
        v_st = p.op(
            "vector",
            lambda e, so=so: e.tensor_scalar_add(
                small[:, so + 2 : so + 3], small[:, so + 1 : so + 2], EPS_LN
            ),
            inc=True,
        )
        p.wait("scalar", "vector", v_st)
        s_sq = p.op(
            "scalar",
            lambda e, so=so: e.sqrt(
                small[:, so + 3 : so + 4], small[:, so + 2 : so + 3]
            ),
            inc=True,
        )
        p.wait("vector", "scalar", s_sq)
        p.op(
            "vector",
            lambda e, so=so: e.reciprocal(
                small[:, so + 4 : so + 5], small[:, so + 3 : so + 4]
            ),
        )
        p.op("vector", lambda e: e.drain())
        v_rb = p.op(
            "vector",
            lambda e, so=so: e.scalar_tensor_tensor(
                small[:, so + 5 : so + 6],
                small[:, so : so + 1],
                -1.0,
                small[:, so + 4 : so + 5],
                ALU.mult,
                ALU.mult,
            ),
            inc=True,
        )
        p.wait("scalar", "vector", v_rb)
        if xt_gate is not None:
            p.wait("scalar", "tensor", xt_gate)
        if extra_wait is not None:
            p.wait("scalar", extra_wait[0], extra_wait[1])
        s_ap = p.op(
            "scalar",
            lambda e, sl=sl, so=so, dst_slot=dst_slot: e.activation(
                xn_tm[:, dst_slot * D : (dst_slot + 1) * D],
                sl,
                AF.Identity,
                bias=small[:, so + 5 : so + 6],
                scale=small[:, so + 4 : so + 5],
            ),
            inc=True,
        )
        xt_free[par] = s_ap
        small_free[par] = s_ap
        return s_ap

    def emit_transpose_pair(a0, a1, dest_fn):
        p.wait("tensor", "scalar", max(a0, a1))
        last_ve, last_se, last_pe = 0, 0, 0
        for i in range(16):
            buf = pA if i % 2 == 0 else pB
            feng, fcnt = ptr_free[i % 2]
            if feng is not None:
                p.wait("tensor", feng, fcnt)
            for jj in range(2):
                tp = p.op(
                    "tensor",
                    lambda e, i=i, jj=jj, buf=buf: e.transpose(
                        buf[:, jj * 128 : jj * 128 + 128],
                        xn_tm[:, jj * D + i * 128 : jj * D + i * 128 + 128],
                        ident[:, :],
                    ),
                    inc=(jj == 1),
                )
            last_pe = tp
            eng = "scalar"
            p.wait(eng, "tensor", tp)
            dst = dest_fn(i)
            c = p.op(
                "scalar",
                lambda e, buf=buf, dst=dst: e.copy(dst, buf[:, 0:256]),
                inc=True,
            )
            last_se = c
            ptr_free[i % 2] = (eng, c)
        return last_ve, last_se, last_pe

    def emit_transpose_pair_ld(dvs, dest_fn):
        p.wait("tensor", "xt0", dvs[0])
        p.wait("tensor", "xt1", dvs[1])
        last_ve, last_se, last_pe = 0, 0, 0
        for i in range(16):
            buf = pA if i % 2 == 0 else pB
            feng, fcnt = ptr_free[i % 2]
            if feng is not None:
                p.wait("tensor", feng, fcnt)
            for jj in range(2):
                tp = p.op(
                    "tensor",
                    lambda e, i=i, jj=jj, buf=buf: e.transpose(
                        buf[:, jj * 128 : jj * 128 + 128],
                        xn_tm[:, jj * D + i * 128 : jj * D + i * 128 + 128],
                        ident[:, :],
                    ),
                    inc=(jj == 1),
                )
            last_pe = tp
            p.wait("scalar", "tensor", tp)
            dst = dest_fn(i)
            c = p.op(
                "scalar",
                lambda e, buf=buf, dst=dst: e.copy(dst, buf[:, 0:256]),
                inc=True,
            )
            last_se = c
            ptr_free[i % 2] = ("scalar", c)
        return last_ve, last_se, last_pe

    p.wait("tensor", "ini", d_ini)
    tp_gate = None
    xnm_done = 0
    st_prev = [0, 0]
    for pr in range(4):
        a0 = emit_ln_tile(
            x_own[pr * 256 : pr * 256 + 128, :], 0, tp_gate,
            ("xns", st_prev[0]) if st_prev[0] else None,
        )
        a1 = emit_ln_tile(
            x_own[pr * 256 + 128 : pr * 256 + 256, :], 1, tp_gate,
            ("xns", st_prev[1]) if st_prev[1] else None,
        )
        p.wait("gpsimd", "scalar", a0)
        st_prev[0] = p.dma_op(
            "gpsimd",
            lambda e, pr=pr: e.dma_start(
                out=xn_m[pr * 256 : pr * 256 + 128, :], in_=xn_tm[:, 0:D]
            ),
            "xns",
        )
        p.wait("gpsimd", "scalar", a1)
        st_prev[1] = p.dma_op(
            "gpsimd",
            lambda e, pr=pr: e.dma_start(
                out=xn_m[pr * 256 + 128 : pr * 256 + 256, :], in_=xn_tm[:, D : 2 * D]
            ),
            "xns",
        )
        xnm_done = st_prev[1]
        ve_e, se_e, tp_gate = emit_transpose_pair(
            a0,
            a1,
            lambda i, pr=pr: xn_c[:, i * TS + pr * 256 : i * TS + pr * 256 + 256],
        )
    own_scalar_end = p.cnt["scalar"]
    p.wait("gpsimd", "xns", xnm_done)
    cc_agxn = p.cc_op(
        lambda e: e.collective_compute(
            "AllGather",
            ALU.bypass,
            replica_groups=[list(range(NC))],
            ins=[xn_m],
            outs=[xn_ag],
        )
    )

    p.wait("vector", "gpsimd", ge_ones)
    scstg_free = [0, 0]
    sc_gated = [False]
    for u in range(24):
        par = u % 2
        if u < 8:
            src = w1_sl[u]
            width, ncol, col = 2048, 4, 0
        else:
            src = w2_sl[u - 8]
            width, ncol, col = 1024, 2, 4
        if not sc_gated[0]:
            p.wait("sync", "scalar", own_scalar_end)
            sc_gated[0] = True
        p.wait("sync", "vector", scstg_free[par])
        dv = p.dma_op(
            "sync",
            lambda e, src=src, width=width, par=par: e.dma_start(
                out=scstg[:, par * D : par * D + width], in_=src
            ),
            "scs%d" % par,
        )
        p.wait("vector", "scs%d" % par, dv)
        acc = 0 if u < 8 else 8
        for half in range(width // 1024):
            p.op(
                "vector",
                lambda e, par=par, half=half: e.tensor_reduce(
                    small[:, 24:32],
                    scstg[
                        :, par * D + half * 1024 : par * D + half * 1024 + 1024
                    ].rearrange("p (c q) -> p c q", q=128),
                    AX.X,
                    ALU.add,
                    apply_absolute_value=True,
                ),
            )
            p.op("vector", lambda e: e.drain())
            scstg_free[par] = p.op(
                "vector",
                lambda e, acc=acc: e.tensor_add(
                    scpart[:, acc : acc + 8],
                    scpart[:, acc : acc + 8],
                    small[:, 24:32],
                ),
                inc=True,
            )
    p.op(
        "vector",
        lambda e: e.tensor_reduce(small[:, 20:21], scpart[:, 0:8], AX.X, ALU.add),
    )
    v_sc = p.op(
        "vector",
        lambda e: e.tensor_reduce(small[:, 21:22], scpart[:, 8:16], AX.X, ALU.add),
        inc=True,
    )

    quant_units = [("w1", i) for i in range(8)] + [
        ("w2", i) for i in range(16)
    ]
    qu_idx = [0]
    qstg_free = [0, 0]
    qout_free = [0, 0]
    quant_gated = [False]

    def emit_quant_units(n):
        for _ in range(n):
            if qu_idx[0] >= len(quant_units):
                return
            u = quant_units[qu_idx[0]]
            qu_idx[0] += 1
            par = qu_idx[0] % 2
            if u[0] == "w1":
                src, dst, width = w1_sl[u[1]], w1q_m[u[1]], 2048
                tcol, ncol = 0, 2
            else:
                i = u[1]
                src, dst, width = w2_sl[i], w2q_m[i // 8, i % 8], 1024
                tcol, ncol = 1, 3
            stg = qstg[:, par * D : par * D + width]
            out = qout[:, 0:width]
            if not quant_gated[0]:
                p.wait("sync", "vector", v_sc)
                quant_gated[0] = True
            p.wait("sync", "vector", qstg_free[par])
            dv = p.dma_op(
                "sync",
                lambda e, stg=stg, src=src: e.dma_start(out=stg, in_=src),
                "qs%d" % par,
            )
            p.wait("vector", "qs%d" % par, dv)
            p.wait("vector", "qo0", qout_free[0])
            p.op(
                "vector",
                lambda e, out=out, stg=stg, ncol=ncol: e.tensor_single_scalar(
                    out, stg, theta[:, ncol : ncol + 1], ALU.is_lt
                ),
            )
            vq = p.op(
                "vector",
                lambda e, out=out, stg=stg, tcol=tcol: e.scalar_tensor_tensor(
                    out, stg, theta[:, tcol : tcol + 1], out, ALU.is_gt, ALU.subtract
                ),
                inc=True,
            )
            qstg_free[par] = vq
            p.wait("sync", "vector", vq)
            dgv = p.dma_op(
                "sync",
                lambda e, dst=dst, out=out: e.dma_start(out=dst, in_=out),
                "qo0",
            )
            qout_free[0] = dgv


    xnstr_free = [0]

    def evac(eng, dst, src_psum, width):
        if eng == "scalar":
            return p.op(
                "scalar",
                lambda e, dst=dst, src_psum=src_psum, width=width: e.copy(
                    dst, src_psum[:, 0:width]
                ),
                inc=True,
            )
        return p.op(
            "vector",
            lambda e, dst=dst, src_psum=src_psum, width=width: e.tensor_copy(
                dst, src_psum[:, 0:width]
            ),
            inc=True,
        )

    p1_end_cell = [0]
    first_t = True
    ld_prev = [0, 0]
    for t in range(NQT):
        for p2 in range(2):
            row = t * 512 + p2 * 256
            if first_t:
                p.wait("scalar", "cc", cc_agxn)
                first_t = False
            dvs = []
            for jj in range(2):
                if tp_gate:
                    p.wait("scalar", "tensor", tp_gate)
                dvs.append(
                    p.dma_op(
                        "scalar",
                        lambda e, row=row, jj=jj: e.dma_start(
                            out=xn_tm[:, jj * D : (jj + 1) * D],
                            in_=xn_ag[row + jj * 128 : row + (jj + 1) * 128, :],
                        ),
                        "xt%d" % jj,
                    )
                )
            if xnstr_free[0]:
                p.wait("scalar", "tensor", xnstr_free[0])
            ve_e, se_e, tp_gate = emit_transpose_pair_ld(
                dvs,
                lambda i, p2=p2: xn_str[:, i * 512 + p2 * 256 : i * 512 + p2 * 256 + 256],
            )
        p.wait("tensor", "vector", ve_e)
        p.wait("tensor", "scalar", se_e)
        if t == 0:
            p.wait("tensor", "vector", cast_done)
            p.wait("scalar", "vector", cast_done)
        last_mm = 0
        for wsb, dstT, pss in ((wq_sb, QTs, (pC, pD)), (wk_sb, KTs, (pE, pF))):
            for dq in range(2):
                ps = pss[dq]
                gate_psum(ps)
                for i in range(16):
                    mm = p.op(
                        "tensor",
                        lambda e, ps=ps, wsb=wsb, dq=dq, i=i: e.matmul(
                            ps[:, 0:512],
                            wsb[:, i * 256 + dq * 128 : i * 256 + dq * 128 + 128],
                            xn_str[:, i * 512 : i * 512 + 512],
                            start=(i == 0),
                            stop=(i == 15),
                        ),
                        inc=(i == 15),
                    )
                last_mm = mm
                eng = "scalar" if dq == 0 else "vector"
                p.wait(eng, "tensor", mm)
                c = evac(
                    eng,
                    dstT[:, dq * S + t * 512 : dq * S + t * 512 + 512],
                    ps,
                    512,
                )
                psum_free[id(ps)] = (eng, c)
        for kc in range(4):
            ps = (pG, pH, pC, pD)[kc]
            eng = ("scalar", "vector", "scalar", "vector")[kc]
            gate_psum(ps)
            for i in range(16):
                mm = p.op(
                    "tensor",
                    lambda e, ps=ps, kc=kc, i=i: e.matmul(
                        ps[:, 0:256],
                        xn_str[:, i * 512 + kc * 128 : i * 512 + kc * 128 + 128],
                        wv_sb[:, i * 256 : i * 256 + 256],
                        start=(i == 0),
                        stop=(i == 15),
                    ),
                    inc=(i == 15),
                )
            last_mm = mm
            p.wait(eng, "tensor", mm)
            c = evac(
                eng, Vtok[:, (4 * t + kc) * 256 : (4 * t + kc) * 256 + 256], ps, 256
            )
            psum_free[id(ps)] = (eng, c)
        xnstr_free[0] = last_mm
        if t == 6:
            gate_psum(pC)
            p.wait("tensor", "vector", v_sc)
            mm_sum = p.op(
                "tensor",
                lambda e: e.matmul(
                    pC[:, 0:2], ones128f[:, :], small[:, 20:22],
                    start=True, stop=True,
                ),
                inc=True,
            )
            p.wait("vector", "tensor", mm_sum)
            v_cp = p.op(
                "vector",
                lambda e: e.tensor_copy(small[:, 22:24], pC[:, 0:2]),
                inc=True,
            )
            psum_free[id(pC)] = ("vector", v_cp)
            p.wait("gpsimd", "vector", v_cp)
            dg_sc = p.dma_op(
                "gpsimd",
                lambda e: e.dma_start(out=sc_in[0:1, 0:2], in_=small[0:1, 22:24]),
                "scx",
            )
            p.wait("gpsimd", "scx", dg_sc)
            cc_sc = p.cc_op(
                lambda e: e.collective_compute(
                    "AllReduce",
                    ALU.add,
                    replica_groups=[list(range(NC))],
                    ins=[sc_in],
                    outs=[sc_out],
                )
            )
            p.wait("gpsimd", "cc", cc_sc)
            dg_sc2 = p.dma_op(
                "gpsimd",
                lambda e: e.dma_start(out=small[0:1, 24:26], in_=sc_out[0:1, 0:2]),
                "scx",
            )
        if t == 10:
            p.wait("tensor", "scx", dg_sc2)
            gate_psum(pD)
            mm_bc = p.op(
                "tensor",
                lambda e: e.matmul(
                    pD[:, 0:2], ones128f[0:1, :], small[0:1, 24:26],
                    start=True, stop=True,
                ),
                inc=True,
            )
            p.wait("vector", "tensor", mm_bc)
            p.op(
                "vector",
                lambda e: e.tensor_scalar(
                    theta[:, 0:2], pD[:, 0:2], float(2.0 ** -25), 0.5e-8,
                    ALU.mult, ALU.max,
                ),
            )
            p.op("vector", lambda e: e.drain())
            v_thn = p.op(
                "vector",
                lambda e: e.tensor_scalar_mul(theta[:, 2:4], theta[:, 0:2], -1.0),
                inc=True,
            )
            psum_free[id(pD)] = ("vector", v_thn)
            emit_quant_units(24)

    ln_done_scalar = p.cnt["scalar"]
    p1_pe_end = p.cnt["tensor"]
    p1_end_cell[0] = p1_pe_end

    if debug_phase >= 1:
        p.wait("gpsimd", "vector", p.cnt["vector"])
        p.wait("gpsimd", "scalar", p.cnt["scalar"])
        p.dma_op("gpsimd", lambda e: e.dma_start(out=dbg["xn_cT"], in_=xn_c[:, :]), "dbg")
        p.dma_op("gpsimd", lambda e: e.dma_start(out=dbg["QT"], in_=QTs[:, :]), "dbg")
        p.dma_op("gpsimd", lambda e: e.dma_start(out=dbg["KT"], in_=KTs[:, :]), "dbg")
        p.dma_op("gpsimd", lambda e: e.dma_start(out=dbg["Vtok"], in_=Vtok[:, :]), "dbg")
    if debug_phase == 1:
        p.wait("gpsimd", "dbg", p.dma["dbg"])
        p.emit(nc)
        return nc

    p.wait("sync", "tensor", p1_pe_end)
    d_masks = p.dma_op(
        "sync",
        lambda e: e.dma_start(out=masks[:, :], in_=masks_h),
        "msk",
    )
    p.op("scalar", lambda e: e.preload_act_table(AF.Exp))
    p.wait("scalar", "tensor", p1_pe_end)
    p.wait("scalar", "qo0", p.dma.get("qo0", 0))
    sq_free = [(None, 0), (None, 0)]
    for h in range(2):
        for which, srcT in (("k", KTs), ("q", QTs)):
            for tq in range(16):
                par = tq % 2
                feng, fcnt = sq_free[par]
                if feng is not None:
                    p.wait("scalar", feng, fcnt)
                ssq = p.op(
                    "scalar",
                    lambda e, h=h, tq=tq, par=par, srcT=srcT: e.square(
                        sq2[:, par * QT : (par + 1) * QT],
                        srcT[:, h * S + tq * 512 : h * S + tq * 512 + 512],
                    ),
                    inc=True,
                )
                ps = pC if par == 0 else pD
                gate_psum(ps)
                p.wait("tensor", "scalar", ssq)
                p.wait("tensor", "gpsimd", ge_ones)
                mm = p.op(
                    "tensor",
                    lambda e, ps=ps, par=par: e.matmul(
                        ps[:, 0:512],
                        ones128[:, :],
                        sq2[:, par * QT : (par + 1) * QT],
                        start=True,
                        stop=True,
                    ),
                    inc=True,
                )
                p.wait("vector", "tensor", mm)
                if which == "q":
                    vr = p.op(
                        "vector",
                        lambda e, ps=ps, h=h, tq=tq: e.reduce_max(
                            qmax[:, h * 16 + tq : h * 16 + tq + 1], ps[:, 0:512], AX.X
                        ),
                        inc=True,
                    )
                    p.op("vector", lambda e: e.drain())
                else:
                    p.op(
                        "vector",
                        lambda e, ps=ps: e.reduce_max(
                            small[:, 28:29], ps[:, 0:512], AX.X
                        ),
                    )
                    p.op("vector", lambda e: e.drain())
                    if tq == 0:
                        vr = p.op(
                            "vector",
                            lambda e, h=h: e.tensor_copy(
                                kmax[:, h : h + 1], small[:, 28:29]
                            ),
                            inc=True,
                        )
                    else:
                        vr = p.op(
                            "vector",
                            lambda e, h=h: e.tensor_max(
                                kmax[:, h : h + 1], kmax[:, h : h + 1], small[:, 28:29]
                            ),
                            inc=True,
                        )
                psum_free[id(ps)] = ("vector", vr)
                sq_free[par] = ("vector", vr)
    v_bias = 0
    for h in range(2):
        p.op(
            "vector",
            lambda e, h=h: e.tensor_scalar_mul(
                small[:, 30:31], kmax[:, h : h + 1], -SCALE / 2.0
            ),
        )
        p.op("vector", lambda e: e.drain())
        for t in range(16):
            v_bias = p.op(
                "vector",
                lambda e, h=h, t=t: e.scalar_tensor_tensor(
                    biasm[:, h * 16 + t : h * 16 + t + 1],
                    qmax[:, h * 16 + t : h * 16 + t + 1],
                    -SCALE / 2.0,
                    small[:, 30:31],
                    ALU.mult,
                    ALU.add,
                ),
                inc=True,
            )

    s_ps = (pC, pD, pE)
    o_ps = {0: pF, 1: pF}
    l_ps = pG
    sp_ps = (pH, pG)
    pT_free = [0, 0, 0]
    sprj_free = [0, 0]
    obar_ready = {}
    mask_gated = [False]
    p.wait("scalar", "vector", v_bias)

    p.wait("gpsimd", "qo0", p.dma.get("qo0", 0))
    p.wait("gpsimd", "tensor", p1_end_cell)
    cc_agw1 = p.cc_op(
        lambda e: e.collective_compute(
            "AllGather", ALU.bypass,
            replica_groups=[list(range(NC))],
            ins=[w1q_m], outs=[w1q_d],
        )
    )
    cc_agw2 = p.cc_op(
        lambda e: e.collective_compute(
            "AllGather", ALU.bypass,
            replica_groups=[list(range(NC))],
            ins=[w2q_m], outs=[w2q_d],
        )
    )

    for t in range(NQT):
        nk = min(4 * t + 5, 64)
        for h in range(2):
            mm_l = 0
            av_gate = {}
            for kc in range(nk + 2):
                if kc < nk:
                    par = kc % 3
                    sps = s_ps[par]
                    gate_psum(sps)
                    mm_s = p.op(
                        "tensor",
                        lambda e, sps=sps, h=h, t=t, kc=kc: e.matmul(
                            sps[:, 0:512],
                            KTs[:, h * S + kc * 128 : h * S + kc * 128 + 128],
                            QTs[:, h * S + t * 512 : h * S + t * 512 + 512],
                            start=True,
                            stop=True,
                        ),
                        inc=True,
                    )
                    p.wait("scalar", "tensor", mm_s)
                    p.wait("scalar", "tensor", pT_free[par])
                    c_exp = p.op(
                        "scalar",
                        lambda e, sps=sps, par=par, h=h, t=t: e.activation(
                            pT[:, par * QT : (par + 1) * QT],
                            sps[:, 0:512],
                            AF.Exp,
                            bias=biasm[:, h * 16 + t : h * 16 + t + 1],
                            scale=SCALE,
                        ),
                        inc=True,
                    )
                    psum_free[id(sps)] = ("scalar", c_exp)
                    gate_eng, gate_cnt = "scalar", c_exp
                    m = kc - 4 * t - 1
                    if 0 <= m <= 3:
                        if not mask_gated[0]:
                            p.wait("vector", "msk", d_masks)
                            mask_gated[0] = True
                        p.wait("vector", "scalar", c_exp)
                        c_msk = p.op(
                            "vector",
                            lambda e, par=par, m=m: e.tensor_mul(
                                pT[:, par * QT : (par + 1) * QT],
                                pT[:, par * QT : (par + 1) * QT],
                                masks[:, m * QT : (m + 1) * QT],
                            ),
                            inc=True,
                        )
                        gate_eng, gate_cnt = "vector", c_msk
                    av_gate[kc] = (gate_eng, gate_cnt)
                if kc >= 2:
                    kd = kc - 2
                    par = kd % 3
                    ge_, gc_ = av_gate.pop(kd)
                    p.wait("tensor", ge_, gc_)
                    if kd == 0:
                        gate_psum(o_ps[h])
                        gate_psum(l_ps)
                    p.op(
                        "tensor",
                        lambda e, h=h, kd=kd, par=par, nk=nk: e.matmul(
                            o_ps[h][:, 0:512],
                            Vtok[:, kd * 256 + h * 128 : kd * 256 + h * 128 + 128],
                            pT[:, par * QT : (par + 1) * QT],
                            start=(kd == 0),
                            stop=(kd == nk - 1),
                        ),
                    )
                    mm_l = p.op(
                        "tensor",
                        lambda e, kd=kd, par=par, nk=nk: e.matmul(
                            l_ps[:, 0:512],
                            ones128[:, :],
                            pT[:, par * QT : (par + 1) * QT],
                            start=(kd == 0),
                            stop=(kd == nk - 1),
                        ),
                        inc=True,
                    )
                    pT_free[par] = mm_l
            p.wait("vector", "tensor", mm_l)
            v_ri = p.op(
                "vector", lambda e: e.reciprocal(rinv[:, :], l_ps[:, 0:512]), inc=True
            )
            psum_free[id(l_ps)] = ("vector", v_ri)
            p.op("vector", lambda e: e.drain())
            v_ob = p.op(
                "vector",
                lambda e, h=h: e.tensor_mul(
                    obar[:, h * QT : (h + 1) * QT], o_ps[h][:, 0:512], rinv[:, :]
                ),
                inc=True,
            )
            psum_free[id(o_ps[h])] = ("vector", v_ob)
            obar_ready[h] = v_ob
        p.wait("tensor", "vector", obar_ready[1])
        for dm in range(16):
            par = dm % 2
            sps = sp_ps[par]
            gate_psum(sps)
            for dc in range(2):
                mm_sp = p.op(
                    "tensor",
                    lambda e, sps=sps, dc=dc, dm=dm: e.matmul(
                        sps[:, 0:512],
                        wo_sb[:, (dc * 16 + dm) * 128 : (dc * 16 + dm) * 128 + 128],
                        obar[:, dc * QT : (dc + 1) * QT],
                        start=(dc == 0),
                        stop=(dc == 1),
                    ),
                    inc=(dc == 1),
                )
            p.wait("vector", "tensor", mm_sp)
            p.wait("vector", "sp%d" % par, sprj_free[par])
            v_sp = p.op(
                "vector",
                lambda e, sps=sps, par=par: e.tensor_copy(
                    sprj[:, par * QT : (par + 1) * QT], sps[:, 0:512]
                ),
                inc=True,
            )
            psum_free[id(sps)] = ("vector", v_sp)
            p.wait("gpsimd", "vector", v_sp)
            dg_sp = p.dma_op(
                "gpsimd",
                lambda e, t=t, dm=dm, par=par: e.dma_start(
                    out=s_part[
                        t // 2,
                        dm * 128 : dm * 128 + 128,
                        (t % 2) * 512 : (t % 2) * 512 + 512,
                    ],
                    in_=sprj[:, par * QT : (par + 1) * QT],
                ),
                "sp%d" % par,
            )
            sprj_free[par] = dg_sp
    p.op("scalar", lambda e: e.preload_act_table(AF.Gelu))
    attn_ve_end = p.cnt["vector"]

    if debug_phase >= 4:
        p.wait("gpsimd", "cc", cc_agw1)
        p.wait("gpsimd", "cc", cc_agw2)
        p.dma_op("gpsimd", lambda e: e.dma_start(out=dbg["w1q"], in_=w1q_d[0:8]), "dbg")
        p.dma_op("gpsimd", lambda e: e.dma_start(out=dbg["w2q"], in_=w2q_d), "dbg")
    p.wait("gpsimd", "sp0", p.dma.get("sp0", 0))
    p.wait("gpsimd", "sp1", p.dma.get("sp1", 0))
    cc_rs = p.cc_op(
        lambda e: e.collective_compute(
            "ReduceScatter",
            ALU.add,
            replica_groups=[list(range(NC))],
            ins=[s_part],
            outs=[s_red],
        )
    )

    if debug_phase >= 2:
        p.dma_op(
            "gpsimd",
            lambda e: e.dma_start(
                out=dbg["spart"], in_=s_part.rearrange("a b c -> (a b) c")
            ),
            "dbg",
        )
        p.dma_op("gpsimd", lambda e: e.dma_start(out=dbg["theta"], in_=theta[:, :]), "dbg")
        p.dma_op("gpsimd", lambda e: e.dma_start(out=dbg["biasm"], in_=biasm[:, :]), "dbg")
    if debug_phase == 2:
        p.wait("gpsimd", "cc", cc_rs)
        p.wait("gpsimd", "dbg", p.dma["dbg"])
        p.emit(nc)
        return nc

    h_psums = (pC, pD)
    y_psums = (pE, pF)
    w1s_free = [0, 0]
    w2s_free = [0, 0]
    ysb_free = {}
    silu_done = {}
    ffn_dma_gated = [False]
    silu_gated = [False]
    pe_a_end = [0]
    hdump_gate = [0]
    ydump_gate = [0]
    for qh in range(2):
        gelu_cnt = {}
        if qh == 1 and hdump_gate[0]:
            p.wait("tensor", "dbg", hdump_gate[0])
        if qh == 1 and ydump_gate[0]:
            p.wait("vector", "dbg", ydump_gate[0])
        for fc in range(64):
            par = fc % 2
            if not ffn_dma_gated[0]:
                p.wait("sync", "vector", attn_ve_end)
                p.wait("sync", "sp0", p.dma.get("sp0", 0))
                p.wait("sync", "sp1", p.dma.get("sp1", 0))
                ffn_dma_gated[0] = True
            p.wait("sync", "tensor", w1s_free[par])
            if fc == 0 and qh == 0:
                p.wait("sync", "cc", cc_agw1)
            dv = p.dma_op(
                "sync",
                lambda e, fc=fc, par=par: e.dma_start(
                    out=w1q_sb[:, par * 2048 : (par + 1) * 2048],
                    in_=w1q_d[fc],
                ),
                "w1l%d" % par,
            )
            ps = h_psums[par]
            gate_psum(ps)
            p.wait("tensor", "w1l%d" % par, dv)
            for i in range(16):
                mm = p.op(
                    "tensor",
                    lambda e, ps=ps, par=par, i=i, qh=qh: e.matmul(
                        ps[:, 0:512],
                        w1q_sb[:, par * 2048 + i * 128 : par * 2048 + i * 128 + 128],
                        xn_c[:, i * TS + qh * 512 : i * TS + qh * 512 + 512],
                        start=(i == 0),
                        stop=(i == 15),
                    ),
                    inc=(i == 15),
                )
            w1s_free[par] = mm
            p.wait("scalar", "tensor", mm)
            if not silu_gated[0]:
                p.wait("scalar", "qo0", p.dma.get("qo0", 0))
                silu_gated[0] = True
            c_g = p.op(
                "scalar",
                lambda e, ps=ps, fc=fc: e.activation(
                    h_sb[:, fc * 512 : fc * 512 + 512],
                    ps[:, 0:512],
                    AF.Gelu,
                    bias=b1s[:, fc : fc + 1],
                ),
                inc=True,
            )
            psum_free[id(ps)] = ("scalar", c_g)
            gelu_cnt[fc] = c_g
        pe_a_end[0] = p.cnt["tensor"]
        if debug_phase >= 4 and qh == 0:
            p.wait("gpsimd", "scalar", gelu_cnt[63])
            dg_h0 = p.dma_op(
                "gpsimd", lambda e: e.dma_start(out=dbg["h0"], in_=h_sb[:, :]), "dbg"
            )
            hdump_gate[0] = dg_h0
        for sblk in range(8):
            for dm in range(16):
                par = (sblk * 16 + dm) % 2
                p.wait("sync", "tensor", w2s_free[par])
                if sblk == 0 and dm == 0 and qh == 0:
                    p.wait("sync", "cc", cc_agw2)
                dv = p.dma_op(
                    "sync",
                    lambda e, dm=dm, sblk=sblk, par=par: e.dma_start(
                        out=w2q_sb[:, par * 1024 : (par + 1) * 1024],
                        in_=w2q_d[dm, sblk],
                    ),
                    "w2l%d" % par,
                )
                ps = y_psums[par]
                gate_psum(ps)
                p.wait("tensor", "w2l%d" % par, dv)
                if dm == 0:
                    p.wait("tensor", "scalar", gelu_cnt[sblk * 8 + 7])
                for j in range(8):
                    fc = sblk * 8 + j
                    mm = p.op(
                        "tensor",
                        lambda e, ps=ps, par=par, j=j, fc=fc: e.matmul(
                            ps[:, 0:512],
                            w2q_sb[:, par * 1024 + j * 128 : par * 1024 + j * 128 + 128],
                            h_sb[:, fc * 512 : fc * 512 + 512],
                            start=(j == 0),
                            stop=(j == 7),
                        ),
                        inc=(j == 7),
                    )
                w2s_free[par] = mm
                p.wait("vector", "tensor", mm)
                if sblk == 0 and dm in ysb_free:
                    p.wait("vector", "scalar", ysb_free[dm])
                if sblk == 0:
                    v_f = p.op(
                        "vector",
                        lambda e, ps=ps, dm=dm: e.tensor_copy(
                            y_sb[:, dm * 512 : dm * 512 + 512], ps[:, 0:512]
                        ),
                        inc=True,
                    )
                else:
                    v_f = p.op(
                        "vector",
                        lambda e, ps=ps, dm=dm: e.tensor_add(
                            y_sb[:, dm * 512 : dm * 512 + 512],
                            y_sb[:, dm * 512 : dm * 512 + 512],
                            ps[:, 0:512],
                        ),
                        inc=True,
                    )
                psum_free[id(ps)] = ("vector", v_f)
                if sblk == 7:
                    p.wait("scalar", "vector", v_f)
                    c_s = p.op(
                        "scalar",
                        lambda e, dm=dm, qh=qh: e.activation(
                            g_sb[:, dm * TS + qh * 512 : dm * TS + qh * 512 + 512],
                            y_sb[:, dm * 512 : dm * 512 + 512],
                            AF.Identity,
                            bias=b2s[:, dm : dm + 1],
                        ),
                        inc=True,
                    )
                    ysb_free[dm] = c_s
                    silu_done[(qh, dm)] = c_s
                    if debug_phase >= 4 and qh == 0 and dm == 15:
                        p.wait("gpsimd", "vector", v_f)
                        ydump_gate[0] = p.dma_op(
                            "gpsimd",
                            lambda e: e.dma_start(out=dbg["y0"], in_=y_sb[:, :]),
                            "dbg",
                        )
    pe_ffn_a_end = p.cnt["tensor"]

    if debug_phase >= 4:
        pass
    gdump_dg = 0
    if debug_phase >= 3:
        p.wait("gpsimd", "scalar", max(silu_done.values()))
        gdump_dg = p.dma_op(
            "gpsimd", lambda e: e.dma_start(out=dbg["g"], in_=g_sb[:, :]), "dbg"
        )

    p.op("scalar", lambda e: e.preload_act_table(AF.Silu))
    if gdump_dg:
        p.wait("scalar", "dbg", gdump_dg)
    silu_p5 = {}
    for dm in range(16):
        for half in range(2):
            p.wait("scalar", "scalar", 0)
            silu_p5[(dm, half)] = p.op(
                "scalar",
                lambda e, dm=dm, half=half: e.activation(
                    g_sb[:, dm * TS + half * 512 : dm * TS + half * 512 + 512],
                    g_sb[:, dm * TS + half * 512 : dm * TS + half * 512 + 512],
                    AF.Silu,
                ),
                inc=True,
            )
    p.wait("sync", "cc", cc_rs)
    p.wait("sync", "tensor", pe_ffn_a_end)
    ssb_free = [0, 0]
    osb_free = [0, 0]
    for dm in range(16):
        for half in range(2):
            par = (dm * 2 + half) % 2
            p.wait("sync", "vector", ssb_free[par])
            dv1 = p.dma_op(
                "sync",
                lambda e, dm=dm, half=half, par=par: e.dma_start(
                    out=s_sb[:, par * 512 : (par + 1) * 512],
                    in_=s_red[dm * 128 : dm * 128 + 128, half * 512 : half * 512 + 512],
                ),
                "p5l%d" % par,
            )
            dv2 = p.dma_op(
                "sync",
                lambda e, dm=dm, half=half, par=par: e.dma_start(
                    out=xr_sb[:, par * 512 : (par + 1) * 512],
                    in_=xT_own[dm * 128 : dm * 128 + 128, half * 512 : half * 512 + 512],
                ),
                "p5l%d" % par,
            )
            p.wait("vector", "p5l%d" % par, dv2)
            p.wait("vector", "scalar", silu_p5[(dm, half)])
            p.wait("vector", "ou%d" % par, osb_free[par])
            p.op(
                "vector",
                lambda e, dm=dm, half=half, par=par: e.tensor_mul(
                    o_sb[:, par * 512 : (par + 1) * 512],
                    s_sb[:, par * 512 : (par + 1) * 512],
                    g_sb[:, dm * TS + half * 512 : dm * TS + half * 512 + 512],
                ),
            )
            v_o = p.op(
                "vector",
                lambda e, par=par: e.tensor_add(
                    o_sb[:, par * 512 : (par + 1) * 512],
                    o_sb[:, par * 512 : (par + 1) * 512],
                    xr_sb[:, par * 512 : (par + 1) * 512],
                ),
                inc=True,
            )
            ssb_free[par] = v_o
            p.wait("gpsimd", "vector", v_o)
            osb_free[par] = p.dma_op(
                "gpsimd",
                lambda e, dm=dm, half=half, par=par: e.dma_start(
                    out=yT[dm * 128 : dm * 128 + 128, half * 512 : half * 512 + 512],
                    in_=o_sb[:, par * 512 : (par + 1) * 512],
                ),
                "ou%d" % par,
            )

    p.wait("gpsimd", "ou0", p.dma.get("ou0", 0))
    p.wait("gpsimd", "ou1", p.dma.get("ou1", 0))
    if "dbg" in p.dma:
        p.wait("gpsimd", "dbg", p.dma["dbg"])
    p.emit(nc)
    return nc




def _prep_inputs(L0_attn_x, L0_attn_wq, L0_attn_wk, L0_attn_wv, L0_attn_wo,
                 L0_ffn_w1, L0_ffn_b1, L0_ffn_w2, L0_ffn_b2):
    import ml_dtypes

    bf = ml_dtypes.bfloat16
    f32 = np.float32
    x = np.ascontiguousarray(np.asarray(L0_attn_x, dtype=f32).reshape(S, D))
    xT = np.ascontiguousarray(x.T)
    wq = np.asarray(L0_attn_wq, dtype=f32)
    wk = np.asarray(L0_attn_wk, dtype=f32)
    wv = np.asarray(L0_attn_wv, dtype=f32)
    wo = np.asarray(L0_attn_wo, dtype=f32)
    w1 = np.asarray(L0_ffn_w1, dtype=f32)
    w2 = np.asarray(L0_ffn_w2, dtype=f32)
    b1 = np.asarray(L0_ffn_b1, dtype=f32)
    b2 = np.asarray(L0_ffn_b2, dtype=f32)

    w1T_b = np.ascontiguousarray(
        w1.T.reshape(16, 128, 64, 128).transpose(2, 1, 0, 3).reshape(64, 128, 2048)
    )
    w2T_b = np.ascontiguousarray(
        w2.T.reshape(8, 8, 128, 16, 128).transpose(3, 0, 2, 1, 4).reshape(16, 8, 128, 1024)
    )
    b1_r = np.ascontiguousarray(b1.reshape(64, 128).T)
    b2_r = np.ascontiguousarray(b2.reshape(16, 128).T)
    ident = np.eye(128, dtype=f32).astype(bf)
    ki = np.arange(128)
    qi = np.arange(QT)
    masks = np.concatenate(
        [(qi[None, :] >= ki[:, None] + 128 * m) for m in range(4)], axis=1
    ).astype(bf)

    in_maps = []
    for c in range(NC):
        hs = 256 * c
        in_maps.append(
            {
                "x_tm": x,
                "x_own": np.ascontiguousarray(x[c * TS : (c + 1) * TS, :]),
                "xT_own": np.ascontiguousarray(xT[:, c * TS : (c + 1) * TS]),
                "wq_c": np.ascontiguousarray(
                    wq[hs : hs + 256, :].T.reshape(16, 128, 256).transpose(1, 0, 2)
                ),
                "wk_c": np.ascontiguousarray(
                    wk[hs : hs + 256, :].T.reshape(16, 128, 256).transpose(1, 0, 2)
                ),
                "wv_c": np.ascontiguousarray(
                    wv[hs : hs + 256, :].T.reshape(16, 128, 256).transpose(1, 0, 2)
                ),
                "wo_c": np.ascontiguousarray(
                    wo.T[hs : hs + 256, :]
                    .reshape(2, 128, 16, 128)
                    .transpose(1, 0, 2, 3)
                    .reshape(128, 32, 128)
                ),
                "w1T_b": w1T_b,
                "w2T_b": w2T_b,
                "w1_sl": np.ascontiguousarray(w1T_b[8 * c : 8 * c + 8]),
                "w2_sl": np.ascontiguousarray(
                    w2T_b[2 * c : 2 * c + 2].reshape(16, 128, 1024)
                ),
                "b1_r": b1_r,
                "b2_r": b2_r,
                "ident_h": ident,
                "masks_h": masks,
            }
        )
    return in_maps


def kernel(L0_attn_x, L0_attn_wq, L0_attn_wk, L0_attn_wv, L0_attn_wo,
           L0_ffn_w1, L0_ffn_b1, L0_ffn_w2, L0_ffn_b2):
    in_maps = _prep_inputs(
        L0_attn_x, L0_attn_wq, L0_attn_wk, L0_attn_wv, L0_attn_wo,
        L0_ffn_w1, L0_ffn_b1, L0_ffn_w2, L0_ffn_b2,
    )
    nc = build_program(0)
    res = run_bass_kernel_spmd(nc, in_maps, core_ids=list(range(NC)))
    yT = np.concatenate([r["yT"] for r in res.results], axis=1)
    return np.ascontiguousarray(yT.T).reshape(1, S, D).astype(np.float32)


# version 12
# speedup vs baseline: 1.0011x; 1.0011x over previous
import numpy as np

import bass_rust
import concourse.bass as bass
import concourse.mybir as mybir
from concourse.bass_utils import run_bass_kernel_spmd

F32 = mybir.dt.float32
BF16 = mybir.dt.bfloat16
AF = mybir.ActivationFunctionType
ALU = mybir.AluOpType
AX = mybir.AxisListType

S = 8192
D = 2048
DH = 128
NC = 8
HPC = 2
TS = S // NC
QT = 512
NQT = S // QT
SCALE = float(DH) ** -0.5
EPS_LN = 1e-5

DEBUG_PHASE = 0


class Plan:
    ENGS = ("tensor", "vector", "scalar", "gpsimd", "sync")

    def __init__(self):
        self.ops = {e: [] for e in self.ENGS}
        self.cnt = {e: 0 for e in self.ENGS}
        self.dma = {}
        self.cc = 0
        self.sems = {}

    def wait(self, eng, sem_name, value):
        if value is None:
            return
        if isinstance(value, list):
            self.ops[eng].append(
                lambda e, s=sem_name, v=value: e.wait_ge(self.sems[s], v[0])
                if v[0] > 0
                else None
            )
            return
        if value <= 0:
            return
        self.ops[eng].append(lambda e, s=sem_name, v=value: e.wait_ge(self.sems[s], v))

    def op(self, eng, fn, inc=False):
        if inc:
            self.cnt[eng] += 1
            v = self.cnt[eng]
            self.ops[eng].append(
                lambda e, f=fn, s=eng: f(e).then_inc(self.sems[s], 1)
            )
            return v
        self.ops[eng].append(fn)
        return None

    def dma_op(self, queue, fn, stream):
        self.dma.setdefault(stream, 0)
        self.dma[stream] += 16
        self.ops[queue].append(
            lambda e, f=fn, n=stream: f(e).then_inc(self.sems[n], 16)
        )
        return self.dma[stream]

    def cc_op(self, fn):
        self.cc += 1
        self.ops["gpsimd"].append(lambda e, f=fn: f(e).then_inc(self.sems["cc"], 1))
        return self.cc

    def emit(self, nc):
        import contextlib

        with contextlib.ExitStack() as st:
            for e in self.ENGS:
                self.sems[e] = st.enter_context(nc.semaphore("sm_" + e))
            for q in self.dma:
                self.sems[q] = st.enter_context(nc.semaphore("sm_" + q))
            self.sems["cc"] = st.enter_context(nc.semaphore("sm_cc"))
            with nc.Block() as block:
                for name, deco in (
                    ("tensor", block.tensor),
                    ("vector", block.vector),
                    ("scalar", block.scalar),
                    ("gpsimd", block.gpsimd),
                    ("sync", block.sync),
                ):
                    ops = self.ops[name]

                    def body(e, ops=ops):
                        for f in ops:
                            f(e)

                    deco(body)


def build_program(debug_phase=None):
    if debug_phase is None:
        debug_phase = DEBUG_PHASE
    nc = bass.Bass("TRN2", target_bir_lowering=False)
    p = Plan()

    dt = nc.dram_tensor
    x_tm = dt("x_tm", [S, D], F32, kind="ExternalInput").ap()
    x_own = dt("x_own", [TS, D], F32, kind="ExternalInput").ap()
    xT_own = dt("xT_own", [D, TS], F32, kind="ExternalInput").ap()
    wq_c = dt("wq_c", [128, 16, 256], F32, kind="ExternalInput").ap()
    wk_c = dt("wk_c", [128, 16, 256], F32, kind="ExternalInput").ap()
    wv_c = dt("wv_c", [128, 16, 256], F32, kind="ExternalInput").ap()
    wo_c = dt("wo_c", [128, 32, 128], F32, kind="ExternalInput").ap()
    w1T_b = dt("w1T_b", [64, 128, 2048], F32, kind="ExternalInput").ap()
    w2T_b = dt("w2T_b", [16, 8, 128, 1024], F32, kind="ExternalInput").ap()
    w1_sl = dt("w1_sl", [8, 128, 2048], F32, kind="ExternalInput").ap()
    w2_sl = dt("w2_sl", [16, 128, 1024], F32, kind="ExternalInput").ap()
    b1_r = dt("b1_r", [128, 64], F32, kind="ExternalInput").ap()
    b2_r = dt("b2_r", [128, 16], F32, kind="ExternalInput").ap()
    ident_h = dt("ident_h", [128, 128], BF16, kind="ExternalInput").ap()
    masks_h = dt("masks_h", [128, 4 * QT], BF16, kind="ExternalInput").ap()

    yT = dt("yT", [D, TS], F32, kind="ExternalOutput").ap()

    w1q_d = dt("w1q_d", [64, 128, 2048], BF16, kind="Internal").ap()
    w2q_d = dt("w2q_d", [16, 8, 128, 1024], BF16, kind="Internal").ap()
    w1q_m = dt("w1q_m", [8, 128, 2048], BF16, kind="Internal").ap()
    w2q_m = dt("w2q_m", [2, 8, 128, 1024], BF16, kind="Internal").ap()
    xn_m = dt("xn_m", [TS, D], BF16, kind="Internal").ap()
    xn_ag = dt("xn_ag", [S, D], BF16, kind="Internal").ap()
    s_part = dt("s_part", [NC, D, TS], F32, kind="Internal").ap()
    s_red = dt("s_red", [D, TS], F32, kind="Internal").ap()
    sc_in = dt("sc_in", [1, 16], F32, kind="Internal").ap()
    sc_out = dt("sc_out", [1, 16], F32, kind="Internal").ap()

    dbg = {}
    if debug_phase >= 1:
        dbg["xn_cT"] = dt("dbg_xn_cT", [128, 16 * TS], BF16, kind="ExternalOutput").ap()
        dbg["QT"] = dt("dbg_QT", [128, HPC * S], BF16, kind="ExternalOutput").ap()
        dbg["KT"] = dt("dbg_KT", [128, HPC * S], BF16, kind="ExternalOutput").ap()
        dbg["Vtok"] = dt("dbg_Vtok", [128, (S // 128) * 256], BF16, kind="ExternalOutput").ap()
    if debug_phase >= 2:
        dbg["spart"] = dt("dbg_spart", [NC * D, TS], F32, kind="ExternalOutput").ap()
        dbg["theta"] = dt("dbg_theta", [128, 4], F32, kind="ExternalOutput").ap()
        dbg["biasm"] = dt("dbg_biasm", [128, 32], F32, kind="ExternalOutput").ap()
    if debug_phase >= 3:
        dbg["g"] = dt("dbg_g", [128, 16 * TS], BF16, kind="ExternalOutput").ap()
    if debug_phase >= 4:
        dbg["h0"] = dt("dbg_h0", [128, 64 * 512], BF16, kind="ExternalOutput").ap()
        dbg["y0"] = dt("dbg_y0", [128, 16 * 512], F32, kind="ExternalOutput").ap()
        dbg["w1q"] = dt("dbg_w1q", [8, 128, 2048], BF16, kind="ExternalOutput").ap()
        dbg["w2q"] = dt("dbg_w2q", [16, 8, 128, 1024], BF16, kind="ExternalOutput").ap()

    K = 1024
    ARENA = 206 * K
    arena = nc.alloc_sbuf_tensor("arena", [128, ARENA], mybir.dt.uint8)
    BASE = None
    for a in nc.m.functions[0].allocations:
        if (
            isinstance(a, mybir.MemoryLocationSet)
            and a.memorylocations
            and a.memorylocations[0].name == "arena"
        ):
            BASE = a.memorylocations[0].addr
    assert BASE is not None and BASE + ARENA <= 229376, BASE
    alloc = nc.alloc_sbuf_tensor_at

    def sb(name, shape, dtyp, at):
        return alloc(name, shape, dtyp, offset=BASE + at)

    xn_c = sb("xn_c", [128, 16 * TS], BF16, 0)
    ident = sb("identsb", [128, 128], BF16, 32 * K)
    ones128 = sb("ones128", [128, 128], BF16, 32 * K + 256)
    b1s = sb("b1s", [128, 64], F32, 32 * K + 512)
    b2s = sb("b2s", [128, 16], F32, 32 * K + 768)
    biasm = sb("biasm", [128, 32], F32, 32 * K + 832)
    qmax = sb("qmax", [128, 32], F32, 32 * K + 960)
    kmax = sb("kmax", [128, 2], F32, 33 * K + 64)
    theta = sb("theta", [128, 4], F32, 33 * K + 96)
    scpart = sb("scpart", [128, 16], F32, 33 * K + 128)
    small = sb("small", [128, 32], F32, 33 * K + 192)
    bst = sb("bst", [128, 48], F32, 33 * K + 320)
    ones128f = sb("ones128f", [128, 128], F32, 33 * K + 512)
    wo_sb = sb("wo_sb", [128, 2 * 16 * 128], BF16, 34 * K)
    wq_sb = sb("wq_sb", [128, 16 * 256], BF16, 42 * K)
    wk_sb = sb("wk_sb", [128, 16 * 256], BF16, 50 * K)
    wv_sb = sb("wv_sb", [128, 16 * 256], BF16, 58 * K)

    A0 = 66 * K
    wstg = sb("wstg", [128, 4 * 4096], F32, A0)
    QTs = sb("QTs", [128, HPC * S], BF16, A0)
    KTs = sb("KTs", [128, HPC * S], BF16, A0 + 32 * K)
    Vtok = sb("Vtok", [128, (S // 128) * 256], BF16, A0 + 64 * K)
    h_sb = sb("h_sb", [128, 64 * 512], BF16, A0)
    y_sb = sb("y_sb", [128, 16 * 512], F32, A0 + 64 * K)

    B0 = A0 + 96 * K
    x_t = sb("x_t", [128, 2 * D], F32, B0)
    xn_tm = sb("xn_tm", [128, 2 * D], BF16, B0 + 16 * K)
    xn_str = sb("xn_str", [128, 16 * QT], BF16, B0 + 24 * K)
    scstg = sb("scstg", [128, 2 * D], F32, B0)
    qstg = sb("qstg", [128, 2 * D], F32, B0)
    qout = sb("qout", [128, D], BF16, B0 + 40 * K)
    pT = sb("pT", [128, 3 * QT], BF16, B0 + 24 * K)
    obar = sb("obar", [128, 2 * QT], BF16, B0 + 27 * K)
    rinv = sb("rinv", [128, QT], F32, B0 + 29 * K)
    sprj = sb("sprj", [128, 2 * QT], F32, B0 + 31 * K)
    masks = sb("maskssb", [128, 4 * QT], BF16, B0 + 35 * K)
    sq2 = sb("sq2", [128, 2 * QT], BF16, B0 + 39 * K)
    g_sb = sb("g_sb", [128, 16 * TS], BF16, B0)
    w1q_sb = sb("w1q_sb", [128, 2 * 2048], BF16, B0 + 32 * K)
    w2q_sb = sb("w2q_sb", [128, 2 * 1024], BF16, B0 + 40 * K)
    s_sb = sb("s_sb", [128, 2 * 512], F32, B0 + 32 * K)
    xr_sb = sb("xr_sb", [128, 2 * 512], F32, B0 + 36 * K)
    o_sb = sb("o_sb", [128, 2 * 512], F32, B0 + 40 * K)
    assert B0 + 44 * K <= ARENA, (B0 + 44 * K, ARENA)

    pal = nc.alloc_psum_tensor
    pA = pal("pA", [128, 1024], BF16)
    pB = pal("pB", [128, 1024], BF16)
    pC = pal("pC", [128, 512], F32)
    pD = pal("pD", [128, 512], F32)
    pE = pal("pE", [128, 512], F32)
    pF = pal("pF", [128, 512], F32)
    pG = pal("pG", [128, 512], F32)
    pH = pal("pH", [128, 512], F32)
    psum_free = {id(ps): (None, 0) for ps in (pC, pD, pE, pF, pG, pH)}
    ptr_free = [(None, 0), (None, 0)]

    def gate_psum(ps):
        eng, cnt = psum_free[id(ps)]
        if eng is not None:
            p.wait("tensor", eng, cnt)

    p.op("gpsimd", lambda e: e.memset(ones128[:, :], 1.0))
    p.op("gpsimd", lambda e: e.memset(ones128f[:, :], 1.0))
    ge_ones = p.op("gpsimd", lambda e: e.memset(scpart[:, :], 0.0), inc=True)

    p.dma_op("sync", lambda e: e.dma_start(out=ident[:, :], in_=ident_h), "ini")
    p.dma_op("sync", lambda e: e.dma_start(out=b1s[:, :], in_=b1_r), "ini")
    p.dma_op("sync", lambda e: e.dma_start(out=b2s[:, :], in_=b2_r), "ini")
    for idx, src in enumerate((wq_c, wk_c, wv_c)):
        p.dma_op(
            "sync",
            lambda e, idx=idx, src=src: e.dma_start(
                out=wstg[:, idx * 4096 : (idx + 1) * 4096],
                in_=src.rearrange("p c f -> p (c f)"),
            ),
            "ini",
        )
    d_ini = p.dma_op(
        "sync",
        lambda e: e.dma_start(
            out=wstg[:, 3 * 4096 : 4 * 4096],
            in_=wo_c.rearrange("p c f -> p (c f)"),
        ),
        "ini",
    )
    p.op("scalar", lambda e: e.preload_act_table(AF.Sqrt))
    p.wait("vector", "ini", d_ini)
    for idx, wdst in enumerate((wq_sb, wk_sb, wv_sb, wo_sb)):
        cast_done = p.op(
            "vector",
            lambda e, idx=idx, wdst=wdst: e.tensor_copy(
                wdst[:, :], wstg[:, idx * 4096 : (idx + 1) * 4096]
            ),
            inc=True,
        )

    xt_free = [0, 0]
    small_free = [0, 0]
    ln_cnt = [0]

    def emit_ln_tile(src_ap, dst_slot, xt_gate=None, extra_wait=None):
        j = ln_cnt[0]
        ln_cnt[0] += 1
        par = j % 2
        sl = x_t[:, par * D : (par + 1) * D]
        p.wait("gpsimd", "scalar", xt_free[par])
        dv = p.dma_op(
            "gpsimd",
            lambda e, src_ap=src_ap, sl=sl: e.dma_start(out=sl, in_=src_ap),
            "xt%d" % par,
        )
        p.wait("vector", "xt%d" % par, dv)
        p.wait("vector", "scalar", small_free[par])
        so = 12 * par
        bo = 24 * par
        for i in range(4):
            p.op(
                "vector",
                lambda e, i=i, sl=sl, bo=bo: e.bn_stats(
                    bst[:, bo + 6 * i : bo + 6 * i + 6],
                    sl[:, 512 * i : 512 * i + 512],
                ),
            )
        p.op("vector", lambda e: e.drain())
        p.op(
            "vector",
            lambda e, so=so, bo=bo: e.bn_aggr(
                small[:, so : so + 2], bst[:, bo : bo + 24]
            ),
        )
        p.op("vector", lambda e: e.drain())
        v_st = p.op(
            "vector",
            lambda e, so=so: e.tensor_scalar_add(
                small[:, so + 2 : so + 3], small[:, so + 1 : so + 2], EPS_LN
            ),
            inc=True,
        )
        p.wait("scalar", "vector", v_st)
        s_sq = p.op(
            "scalar",
            lambda e, so=so: e.sqrt(
                small[:, so + 3 : so + 4], small[:, so + 2 : so + 3]
            ),
            inc=True,
        )
        p.wait("vector", "scalar", s_sq)
        p.op(
            "vector",
            lambda e, so=so: e.reciprocal(
                small[:, so + 4 : so + 5], small[:, so + 3 : so + 4]
            ),
        )
        p.op("vector", lambda e: e.drain())
        v_rb = p.op(
            "vector",
            lambda e, so=so: e.scalar_tensor_tensor(
                small[:, so + 5 : so + 6],
                small[:, so : so + 1],
                -1.0,
                small[:, so + 4 : so + 5],
                ALU.mult,
                ALU.mult,
            ),
            inc=True,
        )
        p.wait("scalar", "vector", v_rb)
        if xt_gate is not None:
            p.wait("scalar", "tensor", xt_gate)
        if extra_wait is not None:
            p.wait("scalar", extra_wait[0], extra_wait[1])
        s_ap = p.op(
            "scalar",
            lambda e, sl=sl, so=so, dst_slot=dst_slot: e.activation(
                xn_tm[:, dst_slot * D : (dst_slot + 1) * D],
                sl,
                AF.Identity,
                bias=small[:, so + 5 : so + 6],
                scale=small[:, so + 4 : so + 5],
            ),
            inc=True,
        )
        xt_free[par] = s_ap
        small_free[par] = s_ap
        return s_ap

    def emit_transpose_pair(a0, a1, dest_fn):
        p.wait("tensor", "scalar", max(a0, a1))
        last_ve, last_se, last_pe = 0, 0, 0
        for i in range(16):
            buf = pA if i % 2 == 0 else pB
            feng, fcnt = ptr_free[i % 2]
            if feng is not None:
                p.wait("tensor", feng, fcnt)
            for jj in range(2):
                tp = p.op(
                    "tensor",
                    lambda e, i=i, jj=jj, buf=buf: e.transpose(
                        buf[:, jj * 128 : jj * 128 + 128],
                        xn_tm[:, jj * D + i * 128 : jj * D + i * 128 + 128],
                        ident[:, :],
                    ),
                    inc=(jj == 1),
                )
            last_pe = tp
            eng = "scalar"
            p.wait(eng, "tensor", tp)
            dst = dest_fn(i)
            c = p.op(
                "scalar",
                lambda e, buf=buf, dst=dst: e.copy(dst, buf[:, 0:256]),
                inc=True,
            )
            last_se = c
            ptr_free[i % 2] = (eng, c)
        return last_ve, last_se, last_pe

    def emit_transpose_pair_ld(dvs, dest_fn):
        p.wait("tensor", "xt0", dvs[0])
        p.wait("tensor", "xt1", dvs[1])
        last_ve, last_se, last_pe = 0, 0, 0
        for i in range(16):
            buf = pA if i % 2 == 0 else pB
            feng, fcnt = ptr_free[i % 2]
            if feng is not None:
                p.wait("tensor", feng, fcnt)
            for jj in range(2):
                tp = p.op(
                    "tensor",
                    lambda e, i=i, jj=jj, buf=buf: e.transpose(
                        buf[:, jj * 128 : jj * 128 + 128],
                        xn_tm[:, jj * D + i * 128 : jj * D + i * 128 + 128],
                        ident[:, :],
                    ),
                    inc=(jj == 1),
                )
            last_pe = tp
            p.wait("scalar", "tensor", tp)
            dst = dest_fn(i)
            c = p.op(
                "scalar",
                lambda e, buf=buf, dst=dst: e.copy(dst, buf[:, 0:256]),
                inc=True,
            )
            last_se = c
            ptr_free[i % 2] = ("scalar", c)
        return last_ve, last_se, last_pe

    p.wait("tensor", "ini", d_ini)
    tp_gate = None
    xnm_done = 0
    st_prev = [0, 0]
    for pr in range(4):
        a0 = emit_ln_tile(
            x_own[pr * 256 : pr * 256 + 128, :], 0, tp_gate,
            ("xns", st_prev[0]) if st_prev[0] else None,
        )
        a1 = emit_ln_tile(
            x_own[pr * 256 + 128 : pr * 256 + 256, :], 1, tp_gate,
            ("xns", st_prev[1]) if st_prev[1] else None,
        )
        p.wait("gpsimd", "scalar", a0)
        st_prev[0] = p.dma_op(
            "gpsimd",
            lambda e, pr=pr: e.dma_start(
                out=xn_m[pr * 256 : pr * 256 + 128, :], in_=xn_tm[:, 0:D]
            ),
            "xns",
        )
        p.wait("gpsimd", "scalar", a1)
        st_prev[1] = p.dma_op(
            "gpsimd",
            lambda e, pr=pr: e.dma_start(
                out=xn_m[pr * 256 + 128 : pr * 256 + 256, :], in_=xn_tm[:, D : 2 * D]
            ),
            "xns",
        )
        xnm_done = st_prev[1]
        ve_e, se_e, tp_gate = emit_transpose_pair(
            a0,
            a1,
            lambda i, pr=pr: xn_c[:, i * TS + pr * 256 : i * TS + pr * 256 + 256],
        )
    own_scalar_end = p.cnt["scalar"]
    p.wait("gpsimd", "xns", xnm_done)
    cc_agxn = p.cc_op(
        lambda e: e.collective_compute(
            "AllGather",
            ALU.bypass,
            replica_groups=[list(range(NC))],
            ins=[xn_m],
            outs=[xn_ag],
        )
    )

    p.wait("vector", "gpsimd", ge_ones)
    scstg_free = [0, 0]
    sc_gated = [False]
    for u in range(24):
        par = u % 2
        if u < 8:
            src = w1_sl[u]
            width, ncol, col = 2048, 4, 0
        else:
            src = w2_sl[u - 8]
            width, ncol, col = 1024, 2, 4
        if not sc_gated[0]:
            p.wait("sync", "scalar", own_scalar_end)
            sc_gated[0] = True
        p.wait("sync", "vector", scstg_free[par])
        dv = p.dma_op(
            "sync",
            lambda e, src=src, width=width, par=par: e.dma_start(
                out=scstg[:, par * D : par * D + width], in_=src
            ),
            "scs%d" % par,
        )
        p.wait("vector", "scs%d" % par, dv)
        acc = 0 if u < 8 else 8
        for half in range(width // 1024):
            p.op(
                "vector",
                lambda e, par=par, half=half: e.tensor_reduce(
                    small[:, 24:32],
                    scstg[
                        :, par * D + half * 1024 : par * D + half * 1024 + 1024
                    ].rearrange("p (c q) -> p c q", q=128),
                    AX.X,
                    ALU.add,
                    apply_absolute_value=True,
                ),
            )
            p.op("vector", lambda e: e.drain())
            scstg_free[par] = p.op(
                "vector",
                lambda e, acc=acc: e.tensor_add(
                    scpart[:, acc : acc + 8],
                    scpart[:, acc : acc + 8],
                    small[:, 24:32],
                ),
                inc=True,
            )
    p.op(
        "vector",
        lambda e: e.tensor_reduce(small[:, 20:21], scpart[:, 0:8], AX.X, ALU.add),
    )
    v_sc = p.op(
        "vector",
        lambda e: e.tensor_reduce(small[:, 21:22], scpart[:, 8:16], AX.X, ALU.add),
        inc=True,
    )

    quant_units = [("w1", i) for i in range(8)] + [
        ("w2", i) for i in range(16)
    ]
    qu_idx = [0]
    qstg_free = [0, 0]
    qout_free = [0, 0]
    quant_gated = [False]

    def emit_quant_units(n):
        for _ in range(n):
            if qu_idx[0] >= len(quant_units):
                return
            u = quant_units[qu_idx[0]]
            qu_idx[0] += 1
            par = qu_idx[0] % 2
            if u[0] == "w1":
                src, dst, width = w1_sl[u[1]], w1q_m[u[1]], 2048
                tcol, ncol = 0, 2
            else:
                i = u[1]
                src, dst, width = w2_sl[i], w2q_m[i // 8, i % 8], 1024
                tcol, ncol = 1, 3
            stg = qstg[:, par * D : par * D + width]
            out = qout[:, 0:width]
            if not quant_gated[0]:
                p.wait("sync", "vector", v_sc)
                quant_gated[0] = True
            p.wait("sync", "vector", qstg_free[par])
            dv = p.dma_op(
                "sync",
                lambda e, stg=stg, src=src: e.dma_start(out=stg, in_=src),
                "qs%d" % par,
            )
            p.wait("vector", "qs%d" % par, dv)
            p.wait("vector", "qo0", qout_free[0])
            p.op(
                "vector",
                lambda e, out=out, stg=stg, ncol=ncol: e.tensor_single_scalar(
                    out, stg, theta[:, ncol : ncol + 1], ALU.is_lt
                ),
            )
            vq = p.op(
                "vector",
                lambda e, out=out, stg=stg, tcol=tcol: e.scalar_tensor_tensor(
                    out, stg, theta[:, tcol : tcol + 1], out, ALU.is_gt, ALU.subtract
                ),
                inc=True,
            )
            qstg_free[par] = vq
            p.wait("sync", "vector", vq)
            dgv = p.dma_op(
                "sync",
                lambda e, dst=dst, out=out: e.dma_start(out=dst, in_=out),
                "qo0",
            )
            qout_free[0] = dgv


    xnstr_free = [0]

    def evac(eng, dst, src_psum, width):
        if eng == "scalar":
            return p.op(
                "scalar",
                lambda e, dst=dst, src_psum=src_psum, width=width: e.copy(
                    dst, src_psum[:, 0:width]
                ),
                inc=True,
            )
        return p.op(
            "vector",
            lambda e, dst=dst, src_psum=src_psum, width=width: e.tensor_copy(
                dst, src_psum[:, 0:width]
            ),
            inc=True,
        )

    p1_end_cell = [0]
    first_t = True
    ld_prev = [0, 0]
    for t in range(NQT):
        for p2 in range(2):
            row = t * 512 + p2 * 256
            if first_t:
                p.wait("scalar", "cc", cc_agxn)
                first_t = False
            dvs = []
            for jj in range(2):
                if tp_gate:
                    p.wait("scalar", "tensor", tp_gate)
                dvs.append(
                    p.dma_op(
                        "scalar",
                        lambda e, row=row, jj=jj: e.dma_start(
                            out=xn_tm[:, jj * D : (jj + 1) * D],
                            in_=xn_ag[row + jj * 128 : row + (jj + 1) * 128, :],
                        ),
                        "xt%d" % jj,
                    )
                )
            if xnstr_free[0]:
                p.wait("scalar", "tensor", xnstr_free[0])
            ve_e, se_e, tp_gate = emit_transpose_pair_ld(
                dvs,
                lambda i, p2=p2: xn_str[:, i * 512 + p2 * 256 : i * 512 + p2 * 256 + 256],
            )
        p.wait("tensor", "vector", ve_e)
        p.wait("tensor", "scalar", se_e)
        if t == 0:
            p.wait("tensor", "vector", cast_done)
            p.wait("scalar", "vector", cast_done)
        last_mm = 0
        for wsb, dstT, pss in ((wq_sb, QTs, (pC, pD)), (wk_sb, KTs, (pE, pF))):
            for dq in range(2):
                ps = pss[dq]
                gate_psum(ps)
                for i in range(16):
                    mm = p.op(
                        "tensor",
                        lambda e, ps=ps, wsb=wsb, dq=dq, i=i: e.matmul(
                            ps[:, 0:512],
                            wsb[:, i * 256 + dq * 128 : i * 256 + dq * 128 + 128],
                            xn_str[:, i * 512 : i * 512 + 512],
                            start=(i == 0),
                            stop=(i == 15),
                        ),
                        inc=(i == 15),
                    )
                last_mm = mm
                eng = "scalar" if dq == 0 else "vector"
                p.wait(eng, "tensor", mm)
                c = evac(
                    eng,
                    dstT[:, dq * S + t * 512 : dq * S + t * 512 + 512],
                    ps,
                    512,
                )
                psum_free[id(ps)] = (eng, c)
        for kc in range(4):
            ps = (pG, pH, pC, pD)[kc]
            eng = ("scalar", "vector", "scalar", "vector")[kc]
            gate_psum(ps)
            for i in range(16):
                mm = p.op(
                    "tensor",
                    lambda e, ps=ps, kc=kc, i=i: e.matmul(
                        ps[:, 0:256],
                        xn_str[:, i * 512 + kc * 128 : i * 512 + kc * 128 + 128],
                        wv_sb[:, i * 256 : i * 256 + 256],
                        start=(i == 0),
                        stop=(i == 15),
                    ),
                    inc=(i == 15),
                )
            last_mm = mm
            p.wait(eng, "tensor", mm)
            c = evac(
                eng, Vtok[:, (4 * t + kc) * 256 : (4 * t + kc) * 256 + 256], ps, 256
            )
            psum_free[id(ps)] = (eng, c)
        xnstr_free[0] = last_mm
        if t == 3:
            gate_psum(pC)
            p.wait("tensor", "vector", v_sc)
            mm_sum = p.op(
                "tensor",
                lambda e: e.matmul(
                    pC[:, 0:2], ones128f[:, :], small[:, 20:22],
                    start=True, stop=True,
                ),
                inc=True,
            )
            p.wait("vector", "tensor", mm_sum)
            v_cp = p.op(
                "vector",
                lambda e: e.tensor_copy(small[:, 22:24], pC[:, 0:2]),
                inc=True,
            )
            psum_free[id(pC)] = ("vector", v_cp)
            p.wait("gpsimd", "vector", v_cp)
            dg_sc = p.dma_op(
                "gpsimd",
                lambda e: e.dma_start(out=sc_in[0:1, 0:2], in_=small[0:1, 22:24]),
                "scx",
            )
            p.wait("gpsimd", "scx", dg_sc)
            cc_sc = p.cc_op(
                lambda e: e.collective_compute(
                    "AllReduce",
                    ALU.add,
                    replica_groups=[list(range(NC))],
                    ins=[sc_in],
                    outs=[sc_out],
                )
            )
            p.wait("gpsimd", "cc", cc_sc)
            dg_sc2 = p.dma_op(
                "gpsimd",
                lambda e: e.dma_start(out=small[0:1, 24:26], in_=sc_out[0:1, 0:2]),
                "scx",
            )
        if t == 14:
            p.wait("tensor", "scx", dg_sc2)
            gate_psum(pD)
            mm_bc = p.op(
                "tensor",
                lambda e: e.matmul(
                    pD[:, 0:2], ones128f[0:1, :], small[0:1, 24:26],
                    start=True, stop=True,
                ),
                inc=True,
            )
            p.wait("vector", "tensor", mm_bc)
            p.op(
                "vector",
                lambda e: e.tensor_scalar(
                    theta[:, 0:2], pD[:, 0:2], float(2.0 ** -25), 0.5e-8,
                    ALU.mult, ALU.max,
                ),
            )
            p.op("vector", lambda e: e.drain())
            v_thn = p.op(
                "vector",
                lambda e: e.tensor_scalar_mul(theta[:, 2:4], theta[:, 0:2], -1.0),
                inc=True,
            )
            psum_free[id(pD)] = ("vector", v_thn)
            emit_quant_units(24)

    ln_done_scalar = p.cnt["scalar"]
    p1_pe_end = p.cnt["tensor"]
    p1_end_cell[0] = p1_pe_end

    if debug_phase >= 1:
        p.wait("gpsimd", "vector", p.cnt["vector"])
        p.wait("gpsimd", "scalar", p.cnt["scalar"])
        p.dma_op("gpsimd", lambda e: e.dma_start(out=dbg["xn_cT"], in_=xn_c[:, :]), "dbg")
        p.dma_op("gpsimd", lambda e: e.dma_start(out=dbg["QT"], in_=QTs[:, :]), "dbg")
        p.dma_op("gpsimd", lambda e: e.dma_start(out=dbg["KT"], in_=KTs[:, :]), "dbg")
        p.dma_op("gpsimd", lambda e: e.dma_start(out=dbg["Vtok"], in_=Vtok[:, :]), "dbg")
    if debug_phase == 1:
        p.wait("gpsimd", "dbg", p.dma["dbg"])
        p.emit(nc)
        return nc

    p.wait("sync", "tensor", p1_pe_end)
    d_masks = p.dma_op(
        "sync",
        lambda e: e.dma_start(out=masks[:, :], in_=masks_h),
        "msk",
    )
    p.op("scalar", lambda e: e.preload_act_table(AF.Exp))
    p.wait("scalar", "tensor", p1_pe_end)
    p.wait("scalar", "qo0", p.dma.get("qo0", 0))
    sq_free = [(None, 0), (None, 0)]
    for h in range(2):
        for which, srcT in (("k", KTs), ("q", QTs)):
            for tq in range(16):
                par = tq % 2
                feng, fcnt = sq_free[par]
                if feng is not None:
                    p.wait("scalar", feng, fcnt)
                ssq = p.op(
                    "scalar",
                    lambda e, h=h, tq=tq, par=par, srcT=srcT: e.square(
                        sq2[:, par * QT : (par + 1) * QT],
                        srcT[:, h * S + tq * 512 : h * S + tq * 512 + 512],
                    ),
                    inc=True,
                )
                ps = pC if par == 0 else pD
                gate_psum(ps)
                p.wait("tensor", "scalar", ssq)
                p.wait("tensor", "gpsimd", ge_ones)
                mm = p.op(
                    "tensor",
                    lambda e, ps=ps, par=par: e.matmul(
                        ps[:, 0:512],
                        ones128[:, :],
                        sq2[:, par * QT : (par + 1) * QT],
                        start=True,
                        stop=True,
                    ),
                    inc=True,
                )
                p.wait("vector", "tensor", mm)
                if which == "q":
                    vr = p.op(
                        "vector",
                        lambda e, ps=ps, h=h, tq=tq: e.reduce_max(
                            qmax[:, h * 16 + tq : h * 16 + tq + 1], ps[:, 0:512], AX.X
                        ),
                        inc=True,
                    )
                    p.op("vector", lambda e: e.drain())
                else:
                    p.op(
                        "vector",
                        lambda e, ps=ps: e.reduce_max(
                            small[:, 28:29], ps[:, 0:512], AX.X
                        ),
                    )
                    p.op("vector", lambda e: e.drain())
                    if tq == 0:
                        vr = p.op(
                            "vector",
                            lambda e, h=h: e.tensor_copy(
                                kmax[:, h : h + 1], small[:, 28:29]
                            ),
                            inc=True,
                        )
                    else:
                        vr = p.op(
                            "vector",
                            lambda e, h=h: e.tensor_max(
                                kmax[:, h : h + 1], kmax[:, h : h + 1], small[:, 28:29]
                            ),
                            inc=True,
                        )
                psum_free[id(ps)] = ("vector", vr)
                sq_free[par] = ("vector", vr)
    v_bias = 0
    for h in range(2):
        p.op(
            "vector",
            lambda e, h=h: e.tensor_scalar_mul(
                small[:, 30:31], kmax[:, h : h + 1], -SCALE / 2.0
            ),
        )
        p.op("vector", lambda e: e.drain())
        for t in range(16):
            v_bias = p.op(
                "vector",
                lambda e, h=h, t=t: e.scalar_tensor_tensor(
                    biasm[:, h * 16 + t : h * 16 + t + 1],
                    qmax[:, h * 16 + t : h * 16 + t + 1],
                    -SCALE / 2.0,
                    small[:, 30:31],
                    ALU.mult,
                    ALU.add,
                ),
                inc=True,
            )

    s_ps = (pC, pD, pE)
    o_ps = {0: pF, 1: pF}
    l_ps = pG
    sp_ps = (pH, pG)
    pT_free = [0, 0, 0]
    sprj_free = [0, 0]
    obar_ready = {}
    mask_gated = [False]
    p.wait("scalar", "vector", v_bias)

    p.wait("gpsimd", "qo0", p.dma.get("qo0", 0))
    p.wait("gpsimd", "tensor", p1_end_cell)
    cc_agw1 = p.cc_op(
        lambda e: e.collective_compute(
            "AllGather", ALU.bypass,
            replica_groups=[list(range(NC))],
            ins=[w1q_m], outs=[w1q_d],
        )
    )
    cc_agw2 = p.cc_op(
        lambda e: e.collective_compute(
            "AllGather", ALU.bypass,
            replica_groups=[list(range(NC))],
            ins=[w2q_m], outs=[w2q_d],
        )
    )

    for t in range(NQT):
        nk = min(4 * t + 5, 64)
        for h in range(2):
            mm_l = 0
            av_gate = {}
            for kc in range(nk + 2):
                if kc < nk:
                    par = kc % 3
                    sps = s_ps[par]
                    gate_psum(sps)
                    mm_s = p.op(
                        "tensor",
                        lambda e, sps=sps, h=h, t=t, kc=kc: e.matmul(
                            sps[:, 0:512],
                            KTs[:, h * S + kc * 128 : h * S + kc * 128 + 128],
                            QTs[:, h * S + t * 512 : h * S + t * 512 + 512],
                            start=True,
                            stop=True,
                        ),
                        inc=True,
                    )
                    p.wait("scalar", "tensor", mm_s)
                    p.wait("scalar", "tensor", pT_free[par])
                    c_exp = p.op(
                        "scalar",
                        lambda e, sps=sps, par=par, h=h, t=t: e.activation(
                            pT[:, par * QT : (par + 1) * QT],
                            sps[:, 0:512],
                            AF.Exp,
                            bias=biasm[:, h * 16 + t : h * 16 + t + 1],
                            scale=SCALE,
                        ),
                        inc=True,
                    )
                    psum_free[id(sps)] = ("scalar", c_exp)
                    gate_eng, gate_cnt = "scalar", c_exp
                    m = kc - 4 * t - 1
                    if 0 <= m <= 3:
                        if not mask_gated[0]:
                            p.wait("vector", "msk", d_masks)
                            mask_gated[0] = True
                        p.wait("vector", "scalar", c_exp)
                        c_msk = p.op(
                            "vector",
                            lambda e, par=par, m=m: e.tensor_mul(
                                pT[:, par * QT : (par + 1) * QT],
                                pT[:, par * QT : (par + 1) * QT],
                                masks[:, m * QT : (m + 1) * QT],
                            ),
                            inc=True,
                        )
                        gate_eng, gate_cnt = "vector", c_msk
                    av_gate[kc] = (gate_eng, gate_cnt)
                if kc >= 2:
                    kd = kc - 2
                    par = kd % 3
                    ge_, gc_ = av_gate.pop(kd)
                    p.wait("tensor", ge_, gc_)
                    if kd == 0:
                        gate_psum(o_ps[h])
                        gate_psum(l_ps)
                    p.op(
                        "tensor",
                        lambda e, h=h, kd=kd, par=par, nk=nk: e.matmul(
                            o_ps[h][:, 0:512],
                            Vtok[:, kd * 256 + h * 128 : kd * 256 + h * 128 + 128],
                            pT[:, par * QT : (par + 1) * QT],
                            start=(kd == 0),
                            stop=(kd == nk - 1),
                        ),
                    )
                    mm_l = p.op(
                        "tensor",
                        lambda e, kd=kd, par=par, nk=nk: e.matmul(
                            l_ps[:, 0:512],
                            ones128[:, :],
                            pT[:, par * QT : (par + 1) * QT],
                            start=(kd == 0),
                            stop=(kd == nk - 1),
                        ),
                        inc=True,
                    )
                    pT_free[par] = mm_l
            p.wait("vector", "tensor", mm_l)
            v_ri = p.op(
                "vector", lambda e: e.reciprocal(rinv[:, :], l_ps[:, 0:512]), inc=True
            )
            psum_free[id(l_ps)] = ("vector", v_ri)
            p.op("vector", lambda e: e.drain())
            v_ob = p.op(
                "vector",
                lambda e, h=h: e.tensor_mul(
                    obar[:, h * QT : (h + 1) * QT], o_ps[h][:, 0:512], rinv[:, :]
                ),
                inc=True,
            )
            psum_free[id(o_ps[h])] = ("vector", v_ob)
            obar_ready[h] = v_ob
        p.wait("tensor", "vector", obar_ready[1])
        for dm in range(16):
            par = dm % 2
            sps = sp_ps[par]
            gate_psum(sps)
            for dc in range(2):
                mm_sp = p.op(
                    "tensor",
                    lambda e, sps=sps, dc=dc, dm=dm: e.matmul(
                        sps[:, 0:512],
                        wo_sb[:, (dc * 16 + dm) * 128 : (dc * 16 + dm) * 128 + 128],
                        obar[:, dc * QT : (dc + 1) * QT],
                        start=(dc == 0),
                        stop=(dc == 1),
                    ),
                    inc=(dc == 1),
                )
            p.wait("vector", "tensor", mm_sp)
            p.wait("vector", "sp%d" % par, sprj_free[par])
            v_sp = p.op(
                "vector",
                lambda e, sps=sps, par=par: e.tensor_copy(
                    sprj[:, par * QT : (par + 1) * QT], sps[:, 0:512]
                ),
                inc=True,
            )
            psum_free[id(sps)] = ("vector", v_sp)
            p.wait("gpsimd", "vector", v_sp)
            dg_sp = p.dma_op(
                "gpsimd",
                lambda e, t=t, dm=dm, par=par: e.dma_start(
                    out=s_part[
                        t // 2,
                        dm * 128 : dm * 128 + 128,
                        (t % 2) * 512 : (t % 2) * 512 + 512,
                    ],
                    in_=sprj[:, par * QT : (par + 1) * QT],
                ),
                "sp%d" % par,
            )
            sprj_free[par] = dg_sp
    p.op("scalar", lambda e: e.preload_act_table(AF.Gelu))
    attn_ve_end = p.cnt["vector"]

    if debug_phase >= 4:
        p.wait("gpsimd", "cc", cc_agw1)
        p.wait("gpsimd", "cc", cc_agw2)
        p.dma_op("gpsimd", lambda e: e.dma_start(out=dbg["w1q"], in_=w1q_d[0:8]), "dbg")
        p.dma_op("gpsimd", lambda e: e.dma_start(out=dbg["w2q"], in_=w2q_d), "dbg")
    p.wait("gpsimd", "sp0", p.dma.get("sp0", 0))
    p.wait("gpsimd", "sp1", p.dma.get("sp1", 0))
    cc_rs = p.cc_op(
        lambda e: e.collective_compute(
            "ReduceScatter",
            ALU.add,
            replica_groups=[list(range(NC))],
            ins=[s_part],
            outs=[s_red],
        )
    )

    if debug_phase >= 2:
        p.dma_op(
            "gpsimd",
            lambda e: e.dma_start(
                out=dbg["spart"], in_=s_part.rearrange("a b c -> (a b) c")
            ),
            "dbg",
        )
        p.dma_op("gpsimd", lambda e: e.dma_start(out=dbg["theta"], in_=theta[:, :]), "dbg")
        p.dma_op("gpsimd", lambda e: e.dma_start(out=dbg["biasm"], in_=biasm[:, :]), "dbg")
    if debug_phase == 2:
        p.wait("gpsimd", "cc", cc_rs)
        p.wait("gpsimd", "dbg", p.dma["dbg"])
        p.emit(nc)
        return nc

    h_psums = (pC, pD)
    y_psums = (pE, pF)
    w1s_free = [0, 0]
    w2s_free = [0, 0]
    ysb_free = {}
    silu_done = {}
    ffn_dma_gated = [False]
    silu_gated = [False]
    pe_a_end = [0]
    hdump_gate = [0]
    ydump_gate = [0]
    for qh in range(2):
        gelu_cnt = {}
        if qh == 1 and hdump_gate[0]:
            p.wait("tensor", "dbg", hdump_gate[0])
        if qh == 1 and ydump_gate[0]:
            p.wait("vector", "dbg", ydump_gate[0])
        for fc in range(64):
            par = fc % 2
            if not ffn_dma_gated[0]:
                p.wait("sync", "vector", attn_ve_end)
                p.wait("sync", "sp0", p.dma.get("sp0", 0))
                p.wait("sync", "sp1", p.dma.get("sp1", 0))
                ffn_dma_gated[0] = True
            p.wait("sync", "tensor", w1s_free[par])
            if fc == 0 and qh == 0:
                p.wait("sync", "cc", cc_agw1)
            dv = p.dma_op(
                "sync",
                lambda e, fc=fc, par=par: e.dma_start(
                    out=w1q_sb[:, par * 2048 : (par + 1) * 2048],
                    in_=w1q_d[fc],
                ),
                "w1l%d" % par,
            )
            ps = h_psums[par]
            gate_psum(ps)
            p.wait("tensor", "w1l%d" % par, dv)
            for i in range(16):
                mm = p.op(
                    "tensor",
                    lambda e, ps=ps, par=par, i=i, qh=qh: e.matmul(
                        ps[:, 0:512],
                        w1q_sb[:, par * 2048 + i * 128 : par * 2048 + i * 128 + 128],
                        xn_c[:, i * TS + qh * 512 : i * TS + qh * 512 + 512],
                        start=(i == 0),
                        stop=(i == 15),
                    ),
                    inc=(i == 15),
                )
            w1s_free[par] = mm
            p.wait("scalar", "tensor", mm)
            if not silu_gated[0]:
                p.wait("scalar", "qo0", p.dma.get("qo0", 0))
                silu_gated[0] = True
            c_g = p.op(
                "scalar",
                lambda e, ps=ps, fc=fc: e.activation(
                    h_sb[:, fc * 512 : fc * 512 + 512],
                    ps[:, 0:512],
                    AF.Gelu,
                    bias=b1s[:, fc : fc + 1],
                ),
                inc=True,
            )
            psum_free[id(ps)] = ("scalar", c_g)
            gelu_cnt[fc] = c_g
        pe_a_end[0] = p.cnt["tensor"]
        if debug_phase >= 4 and qh == 0:
            p.wait("gpsimd", "scalar", gelu_cnt[63])
            dg_h0 = p.dma_op(
                "gpsimd", lambda e: e.dma_start(out=dbg["h0"], in_=h_sb[:, :]), "dbg"
            )
            hdump_gate[0] = dg_h0
        for sblk in range(8):
            for dm in range(16):
                par = (sblk * 16 + dm) % 2
                p.wait("sync", "tensor", w2s_free[par])
                if sblk == 0 and dm == 0 and qh == 0:
                    p.wait("sync", "cc", cc_agw2)
                dv = p.dma_op(
                    "sync",
                    lambda e, dm=dm, sblk=sblk, par=par: e.dma_start(
                        out=w2q_sb[:, par * 1024 : (par + 1) * 1024],
                        in_=w2q_d[dm, sblk],
                    ),
                    "w2l%d" % par,
                )
                ps = y_psums[par]
                gate_psum(ps)
                p.wait("tensor", "w2l%d" % par, dv)
                if dm == 0:
                    p.wait("tensor", "scalar", gelu_cnt[sblk * 8 + 7])
                for j in range(8):
                    fc = sblk * 8 + j
                    mm = p.op(
                        "tensor",
                        lambda e, ps=ps, par=par, j=j, fc=fc: e.matmul(
                            ps[:, 0:512],
                            w2q_sb[:, par * 1024 + j * 128 : par * 1024 + j * 128 + 128],
                            h_sb[:, fc * 512 : fc * 512 + 512],
                            start=(j == 0),
                            stop=(j == 7),
                        ),
                        inc=(j == 7),
                    )
                w2s_free[par] = mm
                p.wait("vector", "tensor", mm)
                if sblk == 0 and dm in ysb_free:
                    p.wait("vector", "scalar", ysb_free[dm])
                if sblk == 0:
                    v_f = p.op(
                        "vector",
                        lambda e, ps=ps, dm=dm: e.tensor_copy(
                            y_sb[:, dm * 512 : dm * 512 + 512], ps[:, 0:512]
                        ),
                        inc=True,
                    )
                else:
                    v_f = p.op(
                        "vector",
                        lambda e, ps=ps, dm=dm: e.tensor_add(
                            y_sb[:, dm * 512 : dm * 512 + 512],
                            y_sb[:, dm * 512 : dm * 512 + 512],
                            ps[:, 0:512],
                        ),
                        inc=True,
                    )
                psum_free[id(ps)] = ("vector", v_f)
                if sblk == 7:
                    p.wait("scalar", "vector", v_f)
                    c_s = p.op(
                        "scalar",
                        lambda e, dm=dm, qh=qh: e.activation(
                            g_sb[:, dm * TS + qh * 512 : dm * TS + qh * 512 + 512],
                            y_sb[:, dm * 512 : dm * 512 + 512],
                            AF.Identity,
                            bias=b2s[:, dm : dm + 1],
                        ),
                        inc=True,
                    )
                    ysb_free[dm] = c_s
                    silu_done[(qh, dm)] = c_s
                    if debug_phase >= 4 and qh == 0 and dm == 15:
                        p.wait("gpsimd", "vector", v_f)
                        ydump_gate[0] = p.dma_op(
                            "gpsimd",
                            lambda e: e.dma_start(out=dbg["y0"], in_=y_sb[:, :]),
                            "dbg",
                        )
    pe_ffn_a_end = p.cnt["tensor"]

    if debug_phase >= 4:
        pass
    gdump_dg = 0
    if debug_phase >= 3:
        p.wait("gpsimd", "scalar", max(silu_done.values()))
        gdump_dg = p.dma_op(
            "gpsimd", lambda e: e.dma_start(out=dbg["g"], in_=g_sb[:, :]), "dbg"
        )

    p.op("scalar", lambda e: e.preload_act_table(AF.Silu))
    if gdump_dg:
        p.wait("scalar", "dbg", gdump_dg)
    silu_p5 = {}
    for dm in range(16):
        for half in range(2):
            p.wait("scalar", "scalar", 0)
            silu_p5[(dm, half)] = p.op(
                "scalar",
                lambda e, dm=dm, half=half: e.activation(
                    g_sb[:, dm * TS + half * 512 : dm * TS + half * 512 + 512],
                    g_sb[:, dm * TS + half * 512 : dm * TS + half * 512 + 512],
                    AF.Silu,
                ),
                inc=True,
            )
    p.wait("sync", "cc", cc_rs)
    p.wait("sync", "tensor", pe_ffn_a_end)
    ssb_free = [0, 0]
    osb_free = [0, 0]
    for dm in range(16):
        for half in range(2):
            par = (dm * 2 + half) % 2
            p.wait("sync", "vector", ssb_free[par])
            dv1 = p.dma_op(
                "sync",
                lambda e, dm=dm, half=half, par=par: e.dma_start(
                    out=s_sb[:, par * 512 : (par + 1) * 512],
                    in_=s_red[dm * 128 : dm * 128 + 128, half * 512 : half * 512 + 512],
                ),
                "p5l%d" % par,
            )
            dv2 = p.dma_op(
                "sync",
                lambda e, dm=dm, half=half, par=par: e.dma_start(
                    out=xr_sb[:, par * 512 : (par + 1) * 512],
                    in_=xT_own[dm * 128 : dm * 128 + 128, half * 512 : half * 512 + 512],
                ),
                "p5l%d" % par,
            )
            p.wait("vector", "p5l%d" % par, dv2)
            p.wait("vector", "scalar", silu_p5[(dm, half)])
            p.wait("vector", "ou%d" % par, osb_free[par])
            p.op(
                "vector",
                lambda e, dm=dm, half=half, par=par: e.tensor_mul(
                    o_sb[:, par * 512 : (par + 1) * 512],
                    s_sb[:, par * 512 : (par + 1) * 512],
                    g_sb[:, dm * TS + half * 512 : dm * TS + half * 512 + 512],
                ),
            )
            v_o = p.op(
                "vector",
                lambda e, par=par: e.tensor_add(
                    o_sb[:, par * 512 : (par + 1) * 512],
                    o_sb[:, par * 512 : (par + 1) * 512],
                    xr_sb[:, par * 512 : (par + 1) * 512],
                ),
                inc=True,
            )
            ssb_free[par] = v_o
            p.wait("gpsimd", "vector", v_o)
            osb_free[par] = p.dma_op(
                "gpsimd",
                lambda e, dm=dm, half=half, par=par: e.dma_start(
                    out=yT[dm * 128 : dm * 128 + 128, half * 512 : half * 512 + 512],
                    in_=o_sb[:, par * 512 : (par + 1) * 512],
                ),
                "ou%d" % par,
            )

    p.wait("gpsimd", "ou0", p.dma.get("ou0", 0))
    p.wait("gpsimd", "ou1", p.dma.get("ou1", 0))
    if "dbg" in p.dma:
        p.wait("gpsimd", "dbg", p.dma["dbg"])
    p.emit(nc)
    return nc




def _prep_inputs(L0_attn_x, L0_attn_wq, L0_attn_wk, L0_attn_wv, L0_attn_wo,
                 L0_ffn_w1, L0_ffn_b1, L0_ffn_w2, L0_ffn_b2):
    import ml_dtypes

    bf = ml_dtypes.bfloat16
    f32 = np.float32
    x = np.ascontiguousarray(np.asarray(L0_attn_x, dtype=f32).reshape(S, D))
    xT = np.ascontiguousarray(x.T)
    wq = np.asarray(L0_attn_wq, dtype=f32)
    wk = np.asarray(L0_attn_wk, dtype=f32)
    wv = np.asarray(L0_attn_wv, dtype=f32)
    wo = np.asarray(L0_attn_wo, dtype=f32)
    w1 = np.asarray(L0_ffn_w1, dtype=f32)
    w2 = np.asarray(L0_ffn_w2, dtype=f32)
    b1 = np.asarray(L0_ffn_b1, dtype=f32)
    b2 = np.asarray(L0_ffn_b2, dtype=f32)

    w1T_b = np.ascontiguousarray(
        w1.T.reshape(16, 128, 64, 128).transpose(2, 1, 0, 3).reshape(64, 128, 2048)
    )
    w2T_b = np.ascontiguousarray(
        w2.T.reshape(8, 8, 128, 16, 128).transpose(3, 0, 2, 1, 4).reshape(16, 8, 128, 1024)
    )
    b1_r = np.ascontiguousarray(b1.reshape(64, 128).T)
    b2_r = np.ascontiguousarray(b2.reshape(16, 128).T)
    ident = np.eye(128, dtype=f32).astype(bf)
    ki = np.arange(128)
    qi = np.arange(QT)
    masks = np.concatenate(
        [(qi[None, :] >= ki[:, None] + 128 * m) for m in range(4)], axis=1
    ).astype(bf)

    in_maps = []
    for c in range(NC):
        hs = 256 * c
        in_maps.append(
            {
                "x_tm": x,
                "x_own": np.ascontiguousarray(x[c * TS : (c + 1) * TS, :]),
                "xT_own": np.ascontiguousarray(xT[:, c * TS : (c + 1) * TS]),
                "wq_c": np.ascontiguousarray(
                    wq[hs : hs + 256, :].T.reshape(16, 128, 256).transpose(1, 0, 2)
                ),
                "wk_c": np.ascontiguousarray(
                    wk[hs : hs + 256, :].T.reshape(16, 128, 256).transpose(1, 0, 2)
                ),
                "wv_c": np.ascontiguousarray(
                    wv[hs : hs + 256, :].T.reshape(16, 128, 256).transpose(1, 0, 2)
                ),
                "wo_c": np.ascontiguousarray(
                    wo.T[hs : hs + 256, :]
                    .reshape(2, 128, 16, 128)
                    .transpose(1, 0, 2, 3)
                    .reshape(128, 32, 128)
                ),
                "w1T_b": w1T_b,
                "w2T_b": w2T_b,
                "w1_sl": np.ascontiguousarray(w1T_b[8 * c : 8 * c + 8]),
                "w2_sl": np.ascontiguousarray(
                    w2T_b[2 * c : 2 * c + 2].reshape(16, 128, 1024)
                ),
                "b1_r": b1_r,
                "b2_r": b2_r,
                "ident_h": ident,
                "masks_h": masks,
            }
        )
    return in_maps


def kernel(L0_attn_x, L0_attn_wq, L0_attn_wk, L0_attn_wv, L0_attn_wo,
           L0_ffn_w1, L0_ffn_b1, L0_ffn_w2, L0_ffn_b2):
    in_maps = _prep_inputs(
        L0_attn_x, L0_attn_wq, L0_attn_wk, L0_attn_wv, L0_attn_wo,
        L0_ffn_w1, L0_ffn_b1, L0_ffn_w2, L0_ffn_b2,
    )
    nc = build_program(0)
    res = run_bass_kernel_spmd(nc, in_maps, core_ids=list(range(NC)))
    yT = np.concatenate([r["yT"] for r in res.results], axis=1)
    return np.ascontiguousarray(yT.T).reshape(1, S, D).astype(np.float32)


# version 14
# speedup vs baseline: 1.0406x; 1.0394x over previous
import numpy as np

import bass_rust
import concourse.bass as bass
import concourse.mybir as mybir
from concourse.bass_utils import run_bass_kernel_spmd

F32 = mybir.dt.float32
BF16 = mybir.dt.bfloat16
AF = mybir.ActivationFunctionType
ALU = mybir.AluOpType
AX = mybir.AxisListType

S = 8192
D = 2048
DH = 128
NC = 8
HPC = 2
TS = S // NC
QT = 512
NQT = S // QT
SCALE = float(DH) ** -0.5
EPS_LN = 1e-5

DEBUG_PHASE = 0


class Plan:
    ENGS = ("tensor", "vector", "scalar", "gpsimd", "sync")

    def __init__(self):
        self.ops = {e: [] for e in self.ENGS}
        self.cnt = {e: 0 for e in self.ENGS}
        self.dma = {}
        self.cc = 0
        self.sems = {}

    def wait(self, eng, sem_name, value):
        if value is None:
            return
        if isinstance(value, list):
            self.ops[eng].append(
                lambda e, s=sem_name, v=value: e.wait_ge(self.sems[s], v[0])
                if v[0] > 0
                else None
            )
            return
        if value <= 0:
            return
        self.ops[eng].append(lambda e, s=sem_name, v=value: e.wait_ge(self.sems[s], v))

    def op(self, eng, fn, inc=False):
        if inc:
            self.cnt[eng] += 1
            v = self.cnt[eng]
            self.ops[eng].append(
                lambda e, f=fn, s=eng: f(e).then_inc(self.sems[s], 1)
            )
            return v
        self.ops[eng].append(fn)
        return None

    def dma_op(self, queue, fn, stream):
        self.dma.setdefault(stream, 0)
        self.dma[stream] += 16
        self.ops[queue].append(
            lambda e, f=fn, n=stream: f(e).then_inc(self.sems[n], 16)
        )
        return self.dma[stream]

    def cc_op(self, fn):
        self.cc += 1
        self.ops["gpsimd"].append(lambda e, f=fn: f(e).then_inc(self.sems["cc"], 1))
        return self.cc

    def emit(self, nc):
        import contextlib

        with contextlib.ExitStack() as st:
            for e in self.ENGS:
                self.sems[e] = st.enter_context(nc.semaphore("sm_" + e))
            for q in self.dma:
                self.sems[q] = st.enter_context(nc.semaphore("sm_" + q))
            self.sems["cc"] = st.enter_context(nc.semaphore("sm_cc"))
            with nc.Block() as block:
                for name, deco in (
                    ("tensor", block.tensor),
                    ("vector", block.vector),
                    ("scalar", block.scalar),
                    ("gpsimd", block.gpsimd),
                    ("sync", block.sync),
                ):
                    ops = self.ops[name]

                    def body(e, ops=ops):
                        for f in ops:
                            f(e)

                    deco(body)


def build_program(debug_phase=None):
    if debug_phase is None:
        debug_phase = DEBUG_PHASE
    nc = bass.Bass("TRN2", target_bir_lowering=False)
    p = Plan()

    dt = nc.dram_tensor
    x_tm = dt("x_tm", [S, D], F32, kind="ExternalInput").ap()
    x_own = dt("x_own", [TS, D], F32, kind="ExternalInput").ap()
    xT_own = dt("xT_own", [D, TS], F32, kind="ExternalInput").ap()
    wq_c = dt("wq_c", [128, 16, 256], F32, kind="ExternalInput").ap()
    wk_c = dt("wk_c", [128, 16, 256], F32, kind="ExternalInput").ap()
    wv_c = dt("wv_c", [128, 16, 256], F32, kind="ExternalInput").ap()
    wo_c = dt("wo_c", [128, 32, 128], F32, kind="ExternalInput").ap()
    w1T_b = dt("w1T_b", [64, 128, 2048], F32, kind="ExternalInput").ap()
    w2T_b = dt("w2T_b", [16, 8, 128, 1024], F32, kind="ExternalInput").ap()
    w1_sl = dt("w1_sl", [8, 128, 2048], F32, kind="ExternalInput").ap()
    w2_sl = dt("w2_sl", [16, 128, 1024], F32, kind="ExternalInput").ap()
    b1_r = dt("b1_r", [128, 64], F32, kind="ExternalInput").ap()
    b2_r = dt("b2_r", [128, 16], F32, kind="ExternalInput").ap()
    ident_h = dt("ident_h", [128, 128], BF16, kind="ExternalInput").ap()
    masks_h = dt("masks_h", [128, 4 * QT], BF16, kind="ExternalInput").ap()

    yT = dt("yT", [D, TS], F32, kind="ExternalOutput").ap()

    w1q_d = dt("w1q_d", [64, 128, 2048], BF16, kind="Internal").ap()
    w2q_d = dt("w2q_d", [16, 8, 128, 1024], BF16, kind="Internal").ap()
    w1q_m = dt("w1q_m", [8, 128, 2048], BF16, kind="Internal").ap()
    w2q_m = dt("w2q_m", [2, 8, 128, 1024], BF16, kind="Internal").ap()
    xn_m = dt("xn_m", [TS, D], BF16, kind="Internal").ap()
    xn_agA = dt("xn_agA", [S // 2, D], BF16, kind="Internal").ap()
    xn_agB = dt("xn_agB", [S // 2, D], BF16, kind="Internal").ap()
    s_part = dt("s_part", [NC, D, TS], F32, kind="Internal").ap()
    s_red = dt("s_red", [D, TS], F32, kind="Internal").ap()
    sc_in = dt("sc_in", [1, 16], F32, kind="Internal").ap()
    sc_out = dt("sc_out", [1, 16], F32, kind="Internal").ap()

    dbg = {}
    if debug_phase >= 1:
        dbg["xn_cT"] = dt("dbg_xn_cT", [128, 16 * TS], BF16, kind="ExternalOutput").ap()
        dbg["QT"] = dt("dbg_QT", [128, HPC * S], BF16, kind="ExternalOutput").ap()
        dbg["KT"] = dt("dbg_KT", [128, HPC * S], BF16, kind="ExternalOutput").ap()
        dbg["Vtok"] = dt("dbg_Vtok", [128, (S // 128) * 256], BF16, kind="ExternalOutput").ap()
    if debug_phase >= 2:
        dbg["spart"] = dt("dbg_spart", [NC * D, TS], F32, kind="ExternalOutput").ap()
        dbg["theta"] = dt("dbg_theta", [128, 4], F32, kind="ExternalOutput").ap()
        dbg["biasm"] = dt("dbg_biasm", [128, 32], F32, kind="ExternalOutput").ap()
    if debug_phase >= 3:
        dbg["g"] = dt("dbg_g", [128, 16 * TS], BF16, kind="ExternalOutput").ap()
    if debug_phase >= 4:
        dbg["h0"] = dt("dbg_h0", [128, 64 * 512], BF16, kind="ExternalOutput").ap()
        dbg["y0"] = dt("dbg_y0", [128, 16 * 512], F32, kind="ExternalOutput").ap()
        dbg["w1q"] = dt("dbg_w1q", [8, 128, 2048], BF16, kind="ExternalOutput").ap()
        dbg["w2q"] = dt("dbg_w2q", [16, 8, 128, 1024], BF16, kind="ExternalOutput").ap()

    K = 1024
    ARENA = 206 * K
    arena = nc.alloc_sbuf_tensor("arena", [128, ARENA], mybir.dt.uint8)
    BASE = None
    for a in nc.m.functions[0].allocations:
        if (
            isinstance(a, mybir.MemoryLocationSet)
            and a.memorylocations
            and a.memorylocations[0].name == "arena"
        ):
            BASE = a.memorylocations[0].addr
    assert BASE is not None and BASE + ARENA <= 229376, BASE
    alloc = nc.alloc_sbuf_tensor_at

    def sb(name, shape, dtyp, at):
        return alloc(name, shape, dtyp, offset=BASE + at)

    xn_c = sb("xn_c", [128, 16 * TS], BF16, 0)
    ident = sb("identsb", [128, 128], BF16, 32 * K)
    ones128 = sb("ones128", [128, 128], BF16, 32 * K + 256)
    b1s = sb("b1s", [128, 64], F32, 32 * K + 512)
    b2s = sb("b2s", [128, 16], F32, 32 * K + 768)
    biasm = sb("biasm", [128, 32], F32, 32 * K + 832)
    qmax = sb("qmax", [128, 32], F32, 32 * K + 960)
    kmax = sb("kmax", [128, 2], F32, 33 * K + 64)
    theta = sb("theta", [128, 4], F32, 33 * K + 96)
    scpart = sb("scpart", [128, 16], F32, 33 * K + 128)
    small = sb("small", [128, 32], F32, 33 * K + 192)
    bst = sb("bst", [128, 48], F32, 33 * K + 320)
    ones128f = sb("ones128f", [128, 128], F32, 33 * K + 512)
    wo_sb = sb("wo_sb", [128, 2 * 16 * 128], BF16, 34 * K)
    wq_sb = sb("wq_sb", [128, 16 * 256], BF16, 42 * K)
    wk_sb = sb("wk_sb", [128, 16 * 256], BF16, 50 * K)
    wv_sb = sb("wv_sb", [128, 16 * 256], BF16, 58 * K)

    A0 = 66 * K
    wstg = sb("wstg", [128, 4 * 4096], F32, A0)
    QTs = sb("QTs", [128, HPC * S], BF16, A0)
    KTs = sb("KTs", [128, HPC * S], BF16, A0 + 32 * K)
    Vtok = sb("Vtok", [128, (S // 128) * 256], BF16, A0 + 64 * K)
    h_sb = sb("h_sb", [128, 64 * 512], BF16, A0)
    y_sb = sb("y_sb", [128, 16 * 512], F32, A0 + 64 * K)

    B0 = A0 + 96 * K
    x_t = sb("x_t", [128, 2 * D], F32, B0)
    xn_tm = sb("xn_tm", [128, 2 * D], BF16, B0 + 16 * K)
    xn_str = sb("xn_str", [128, 16 * QT], BF16, B0 + 24 * K)
    scstg = sb("scstg", [128, 2 * D], F32, B0)
    qstg = sb("qstg", [128, 2 * D], F32, B0)
    qout = sb("qout", [128, D], BF16, B0 + 40 * K)
    pT = sb("pT", [128, 3 * QT], BF16, B0 + 24 * K)
    obar = sb("obar", [128, 2 * QT], BF16, B0 + 27 * K)
    rinv = sb("rinv", [128, QT], F32, B0 + 29 * K)
    sprj = sb("sprj", [128, 2 * QT], F32, B0 + 31 * K)
    masks = sb("maskssb", [128, 4 * QT], BF16, B0 + 35 * K)
    sq2 = sb("sq2", [128, 2 * QT], BF16, B0 + 39 * K)
    g_sb = sb("g_sb", [128, 16 * TS], BF16, B0)
    w1q_sb = sb("w1q_sb", [128, 2 * 2048], BF16, B0 + 32 * K)
    w2q_sb = sb("w2q_sb", [128, 2 * 1024], BF16, B0 + 40 * K)
    s_sb = sb("s_sb", [128, 2 * 512], F32, B0 + 32 * K)
    xr_sb = sb("xr_sb", [128, 2 * 512], F32, B0 + 36 * K)
    o_sb = sb("o_sb", [128, 2 * 512], F32, B0 + 40 * K)
    assert B0 + 44 * K <= ARENA, (B0 + 44 * K, ARENA)

    pal = nc.alloc_psum_tensor
    pA = pal("pA", [128, 1024], BF16)
    pB = pal("pB", [128, 1024], BF16)
    pC = pal("pC", [128, 512], F32)
    pD = pal("pD", [128, 512], F32)
    pE = pal("pE", [128, 512], F32)
    pF = pal("pF", [128, 512], F32)
    pG = pal("pG", [128, 512], F32)
    pH = pal("pH", [128, 512], F32)
    psum_free = {id(ps): (None, 0) for ps in (pC, pD, pE, pF, pG, pH)}
    ptr_free = [(None, 0), (None, 0)]

    def gate_psum(ps):
        eng, cnt = psum_free[id(ps)]
        if eng is not None:
            p.wait("tensor", eng, cnt)

    p.op("gpsimd", lambda e: e.memset(ones128[:, :], 1.0))
    p.op("gpsimd", lambda e: e.memset(ones128f[:, :], 1.0))
    ge_ones = p.op("gpsimd", lambda e: e.memset(scpart[:, :], 0.0), inc=True)

    p.dma_op("sync", lambda e: e.dma_start(out=ident[:, :], in_=ident_h), "ini")
    p.dma_op("sync", lambda e: e.dma_start(out=b1s[:, :], in_=b1_r), "ini")
    p.dma_op("sync", lambda e: e.dma_start(out=b2s[:, :], in_=b2_r), "ini")
    for idx, src in enumerate((wq_c, wk_c, wv_c)):
        p.dma_op(
            "sync",
            lambda e, idx=idx, src=src: e.dma_start(
                out=wstg[:, idx * 4096 : (idx + 1) * 4096],
                in_=src.rearrange("p c f -> p (c f)"),
            ),
            "ini",
        )
    d_ini = p.dma_op(
        "sync",
        lambda e: e.dma_start(
            out=wstg[:, 3 * 4096 : 4 * 4096],
            in_=wo_c.rearrange("p c f -> p (c f)"),
        ),
        "ini",
    )
    p.op("scalar", lambda e: e.preload_act_table(AF.Sqrt))
    p.wait("vector", "ini", d_ini)
    for idx, wdst in enumerate((wq_sb, wk_sb, wv_sb, wo_sb)):
        cast_done = p.op(
            "vector",
            lambda e, idx=idx, wdst=wdst: e.tensor_copy(
                wdst[:, :], wstg[:, idx * 4096 : (idx + 1) * 4096]
            ),
            inc=True,
        )

    xt_free = [0, 0]
    small_free = [0, 0]
    ln_cnt = [0]

    def emit_ln_tile(src_ap, dst_slot, xt_gate=None, extra_wait=None):
        j = ln_cnt[0]
        ln_cnt[0] += 1
        par = j % 2
        sl = x_t[:, par * D : (par + 1) * D]
        p.wait("gpsimd", "scalar", xt_free[par])
        dv = p.dma_op(
            "gpsimd",
            lambda e, src_ap=src_ap, sl=sl: e.dma_start(out=sl, in_=src_ap),
            "xt%d" % par,
        )
        p.wait("vector", "xt%d" % par, dv)
        p.wait("vector", "scalar", small_free[par])
        so = 12 * par
        bo = 24 * par
        for i in range(4):
            p.op(
                "vector",
                lambda e, i=i, sl=sl, bo=bo: e.bn_stats(
                    bst[:, bo + 6 * i : bo + 6 * i + 6],
                    sl[:, 512 * i : 512 * i + 512],
                ),
            )
        p.op("vector", lambda e: e.drain())
        p.op(
            "vector",
            lambda e, so=so, bo=bo: e.bn_aggr(
                small[:, so : so + 2], bst[:, bo : bo + 24]
            ),
        )
        p.op("vector", lambda e: e.drain())
        v_st = p.op(
            "vector",
            lambda e, so=so: e.tensor_scalar_add(
                small[:, so + 2 : so + 3], small[:, so + 1 : so + 2], EPS_LN
            ),
            inc=True,
        )
        p.wait("scalar", "vector", v_st)
        s_sq = p.op(
            "scalar",
            lambda e, so=so: e.sqrt(
                small[:, so + 3 : so + 4], small[:, so + 2 : so + 3]
            ),
            inc=True,
        )
        p.wait("vector", "scalar", s_sq)
        p.op(
            "vector",
            lambda e, so=so: e.reciprocal(
                small[:, so + 4 : so + 5], small[:, so + 3 : so + 4]
            ),
        )
        p.op("vector", lambda e: e.drain())
        v_rb = p.op(
            "vector",
            lambda e, so=so: e.scalar_tensor_tensor(
                small[:, so + 5 : so + 6],
                small[:, so : so + 1],
                -1.0,
                small[:, so + 4 : so + 5],
                ALU.mult,
                ALU.mult,
            ),
            inc=True,
        )
        p.wait("scalar", "vector", v_rb)
        if xt_gate is not None:
            p.wait("scalar", "tensor", xt_gate)
        if extra_wait is not None:
            p.wait("scalar", extra_wait[0], extra_wait[1])
        s_ap = p.op(
            "scalar",
            lambda e, sl=sl, so=so, dst_slot=dst_slot: e.activation(
                xn_tm[:, dst_slot * D : (dst_slot + 1) * D],
                sl,
                AF.Identity,
                bias=small[:, so + 5 : so + 6],
                scale=small[:, so + 4 : so + 5],
            ),
            inc=True,
        )
        xt_free[par] = s_ap
        small_free[par] = s_ap
        return s_ap

    def emit_transpose_pair(a0, a1, dest_fn):
        p.wait("tensor", "scalar", max(a0, a1))
        last_ve, last_se, last_pe = 0, 0, 0
        for i in range(16):
            buf = pA if i % 2 == 0 else pB
            feng, fcnt = ptr_free[i % 2]
            if feng is not None:
                p.wait("tensor", feng, fcnt)
            for jj in range(2):
                tp = p.op(
                    "tensor",
                    lambda e, i=i, jj=jj, buf=buf: e.transpose(
                        buf[:, jj * 128 : jj * 128 + 128],
                        xn_tm[:, jj * D + i * 128 : jj * D + i * 128 + 128],
                        ident[:, :],
                    ),
                    inc=(jj == 1),
                )
            last_pe = tp
            eng = "scalar"
            p.wait(eng, "tensor", tp)
            dst = dest_fn(i)
            c = p.op(
                "scalar",
                lambda e, buf=buf, dst=dst: e.copy(dst, buf[:, 0:256]),
                inc=True,
            )
            last_se = c
            ptr_free[i % 2] = (eng, c)
        return last_ve, last_se, last_pe

    def emit_transpose_pair_ld(dvs, dest_fn):
        p.wait("tensor", "xt0", dvs[0])
        p.wait("tensor", "xt1", dvs[1])
        last_ve, last_se, last_pe = 0, 0, 0
        for i in range(16):
            buf = pA if i % 2 == 0 else pB
            feng, fcnt = ptr_free[i % 2]
            if feng is not None:
                p.wait("tensor", feng, fcnt)
            for jj in range(2):
                tp = p.op(
                    "tensor",
                    lambda e, i=i, jj=jj, buf=buf: e.transpose(
                        buf[:, jj * 128 : jj * 128 + 128],
                        xn_tm[:, jj * D + i * 128 : jj * D + i * 128 + 128],
                        ident[:, :],
                    ),
                    inc=(jj == 1),
                )
            last_pe = tp
            p.wait("scalar", "tensor", tp)
            dst = dest_fn(i)
            c = p.op(
                "scalar",
                lambda e, buf=buf, dst=dst: e.copy(dst, buf[:, 0:256]),
                inc=True,
            )
            last_se = c
            ptr_free[i % 2] = ("scalar", c)
        return last_ve, last_se, last_pe

    p.wait("tensor", "ini", d_ini)
    tp_gate = None
    xnm_done = 0
    st_prev = [0, 0]
    for pr in range(4):
        a0 = emit_ln_tile(
            x_own[pr * 256 : pr * 256 + 128, :], 0, tp_gate,
            ("xns", st_prev[0]) if st_prev[0] else None,
        )
        a1 = emit_ln_tile(
            x_own[pr * 256 + 128 : pr * 256 + 256, :], 1, tp_gate,
            ("xns", st_prev[1]) if st_prev[1] else None,
        )
        p.wait("gpsimd", "scalar", a0)
        st_prev[0] = p.dma_op(
            "gpsimd",
            lambda e, pr=pr: e.dma_start(
                out=xn_m[pr * 256 : pr * 256 + 128, :], in_=xn_tm[:, 0:D]
            ),
            "xns",
        )
        p.wait("gpsimd", "scalar", a1)
        st_prev[1] = p.dma_op(
            "gpsimd",
            lambda e, pr=pr: e.dma_start(
                out=xn_m[pr * 256 + 128 : pr * 256 + 256, :], in_=xn_tm[:, D : 2 * D]
            ),
            "xns",
        )
        xnm_done = st_prev[1]
        if pr == 1:
            xnm_half = xnm_done
        ve_e, se_e, tp_gate = emit_transpose_pair(
            a0,
            a1,
            lambda i, pr=pr: xn_c[:, i * TS + pr * 256 : i * TS + pr * 256 + 256],
        )
    own_scalar_end = p.cnt["scalar"]
    p.wait("gpsimd", "xns", xnm_half)
    cc_agxnA = p.cc_op(
        lambda e: e.collective_compute(
            "AllGather",
            ALU.bypass,
            replica_groups=[list(range(NC))],
            ins=[xn_m[0:512, :]],
            outs=[xn_agA],
        )
    )
    p.wait("gpsimd", "xns", xnm_done)
    cc_agxnB = p.cc_op(
        lambda e: e.collective_compute(
            "AllGather",
            ALU.bypass,
            replica_groups=[list(range(NC))],
            ins=[xn_m[512:1024, :]],
            outs=[xn_agB],
        )
    )

    p.wait("vector", "gpsimd", ge_ones)
    scstg_free = [0, 0]
    sc_gated = [False]
    for u in range(24):
        par = u % 2
        if u < 8:
            src = w1_sl[u]
            width, ncol, col = 2048, 4, 0
        else:
            src = w2_sl[u - 8]
            width, ncol, col = 1024, 2, 4
        if not sc_gated[0]:
            p.wait("sync", "scalar", own_scalar_end)
            sc_gated[0] = True
        p.wait("sync", "vector", scstg_free[par])
        dv = p.dma_op(
            "sync",
            lambda e, src=src, width=width, par=par: e.dma_start(
                out=scstg[:, par * D : par * D + width], in_=src
            ),
            "scs%d" % par,
        )
        p.wait("vector", "scs%d" % par, dv)
        acc = 0 if u < 8 else 8
        for half in range(width // 1024):
            p.op(
                "vector",
                lambda e, par=par, half=half: e.tensor_reduce(
                    small[:, 24:32],
                    scstg[
                        :, par * D + half * 1024 : par * D + half * 1024 + 1024
                    ].rearrange("p (c q) -> p c q", q=128),
                    AX.X,
                    ALU.add,
                    apply_absolute_value=True,
                ),
            )
            p.op("vector", lambda e: e.drain())
            scstg_free[par] = p.op(
                "vector",
                lambda e, acc=acc: e.tensor_add(
                    scpart[:, acc : acc + 8],
                    scpart[:, acc : acc + 8],
                    small[:, 24:32],
                ),
                inc=True,
            )
    p.op(
        "vector",
        lambda e: e.tensor_reduce(small[:, 20:21], scpart[:, 0:8], AX.X, ALU.add),
    )
    v_sc = p.op(
        "vector",
        lambda e: e.tensor_reduce(small[:, 21:22], scpart[:, 8:16], AX.X, ALU.add),
        inc=True,
    )

    quant_units = [("w1", i) for i in range(8)] + [
        ("w2", i) for i in range(16)
    ]
    qu_idx = [0]
    qstg_free = [0, 0]
    qout_free = [0, 0]
    quant_gated = [False]

    def emit_quant_units(n):
        for _ in range(n):
            if qu_idx[0] >= len(quant_units):
                return
            u = quant_units[qu_idx[0]]
            qu_idx[0] += 1
            par = qu_idx[0] % 2
            if u[0] == "w1":
                src, dst, width = w1_sl[u[1]], w1q_m[u[1]], 2048
                tcol, ncol = 0, 2
            else:
                i = u[1]
                src, dst, width = w2_sl[i], w2q_m[i // 8, i % 8], 1024
                tcol, ncol = 1, 3
            stg = qstg[:, par * D : par * D + width]
            out = qout[:, 0:width]
            if not quant_gated[0]:
                p.wait("sync", "vector", v_sc)
                quant_gated[0] = True
            p.wait("sync", "vector", qstg_free[par])
            dv = p.dma_op(
                "sync",
                lambda e, stg=stg, src=src: e.dma_start(out=stg, in_=src),
                "qs%d" % par,
            )
            p.wait("vector", "qs%d" % par, dv)
            p.wait("vector", "qo0", qout_free[0])
            p.op(
                "vector",
                lambda e, out=out, stg=stg, ncol=ncol: e.tensor_single_scalar(
                    out, stg, theta[:, ncol : ncol + 1], ALU.is_lt
                ),
            )
            vq = p.op(
                "vector",
                lambda e, out=out, stg=stg, tcol=tcol: e.scalar_tensor_tensor(
                    out, stg, theta[:, tcol : tcol + 1], out, ALU.is_gt, ALU.subtract
                ),
                inc=True,
            )
            qstg_free[par] = vq
            p.wait("sync", "vector", vq)
            dgv = p.dma_op(
                "sync",
                lambda e, dst=dst, out=out: e.dma_start(out=dst, in_=out),
                "qo0",
            )
            qout_free[0] = dgv


    xnstr_free = [0]

    def evac(eng, dst, src_psum, width):
        if eng == "scalar":
            return p.op(
                "scalar",
                lambda e, dst=dst, src_psum=src_psum, width=width: e.copy(
                    dst, src_psum[:, 0:width]
                ),
                inc=True,
            )
        return p.op(
            "vector",
            lambda e, dst=dst, src_psum=src_psum, width=width: e.tensor_copy(
                dst, src_psum[:, 0:width]
            ),
            inc=True,
        )

    first_t = True
    ld_prev = [0, 0]
    for t in range(NQT):
        for p2 in range(2):
            row = t * 512 + p2 * 256
            first_t = False
            dvs = []
            for jj in range(2):
                g = row + jj * 128
                r_blk, off = g // 1024, g % 1024
                if off < 512:
                    srcap = xn_agA[r_blk * 512 + off : r_blk * 512 + off + 128, :]
                    ccv = cc_agxnA
                else:
                    srcap = xn_agB[r_blk * 512 + off - 512 : r_blk * 512 + off - 384, :]
                    ccv = cc_agxnB
                if tp_gate:
                    p.wait("gpsimd", "tensor", tp_gate)
                p.wait("gpsimd", "cc", ccv)
                dvs.append(
                    p.dma_op(
                        "gpsimd",
                        lambda e, srcap=srcap, jj=jj: e.dma_start(
                            out=xn_tm[:, jj * D : (jj + 1) * D],
                            in_=srcap,
                        ),
                        "xt%d" % jj,
                    )
                )
            if xnstr_free[0]:
                p.wait("scalar", "tensor", xnstr_free[0])
            ve_e, se_e, tp_gate = emit_transpose_pair_ld(
                dvs,
                lambda i, p2=p2: xn_str[:, i * 512 + p2 * 256 : i * 512 + p2 * 256 + 256],
            )
        p.wait("tensor", "vector", ve_e)
        p.wait("tensor", "scalar", se_e)
        if t == 0:
            p.wait("tensor", "vector", cast_done)
            p.wait("scalar", "vector", cast_done)
        last_mm = 0
        for wsb, dstT, pss in ((wq_sb, QTs, (pC, pD)), (wk_sb, KTs, (pE, pF))):
            for dq in range(2):
                ps = pss[dq]
                gate_psum(ps)
                for i in range(16):
                    mm = p.op(
                        "tensor",
                        lambda e, ps=ps, wsb=wsb, dq=dq, i=i: e.matmul(
                            ps[:, 0:512],
                            wsb[:, i * 256 + dq * 128 : i * 256 + dq * 128 + 128],
                            xn_str[:, i * 512 : i * 512 + 512],
                            start=(i == 0),
                            stop=(i == 15),
                        ),
                        inc=(i == 15),
                    )
                last_mm = mm
                eng = "scalar" if dq == 0 else "vector"
                p.wait(eng, "tensor", mm)
                c = evac(
                    eng,
                    dstT[:, dq * S + t * 512 : dq * S + t * 512 + 512],
                    ps,
                    512,
                )
                psum_free[id(ps)] = (eng, c)
        for kc in range(4):
            ps = (pG, pH, pC, pD)[kc]
            eng = ("scalar", "vector", "scalar", "vector")[kc]
            gate_psum(ps)
            for i in range(16):
                mm = p.op(
                    "tensor",
                    lambda e, ps=ps, kc=kc, i=i: e.matmul(
                        ps[:, 0:256],
                        xn_str[:, i * 512 + kc * 128 : i * 512 + kc * 128 + 128],
                        wv_sb[:, i * 256 : i * 256 + 256],
                        start=(i == 0),
                        stop=(i == 15),
                    ),
                    inc=(i == 15),
                )
            last_mm = mm
            p.wait(eng, "tensor", mm)
            c = evac(
                eng, Vtok[:, (4 * t + kc) * 256 : (4 * t + kc) * 256 + 256], ps, 256
            )
            psum_free[id(ps)] = (eng, c)
        xnstr_free[0] = last_mm
        if t == 6:
            gate_psum(pC)
            p.wait("tensor", "vector", v_sc)
            mm_sum = p.op(
                "tensor",
                lambda e: e.matmul(
                    pC[:, 0:2], ones128f[:, :], small[:, 20:22],
                    start=True, stop=True,
                ),
                inc=True,
            )
            p.wait("vector", "tensor", mm_sum)
            v_cp = p.op(
                "vector",
                lambda e: e.tensor_copy(small[:, 22:24], pC[:, 0:2]),
                inc=True,
            )
            psum_free[id(pC)] = ("vector", v_cp)
            p.wait("gpsimd", "vector", v_cp)
            dg_sc = p.dma_op(
                "gpsimd",
                lambda e: e.dma_start(out=sc_in[0:1, 0:2], in_=small[0:1, 22:24]),
                "scx",
            )
            p.wait("gpsimd", "scx", dg_sc)
            cc_sc = p.cc_op(
                lambda e: e.collective_compute(
                    "AllReduce",
                    ALU.add,
                    replica_groups=[list(range(NC))],
                    ins=[sc_in],
                    outs=[sc_out],
                )
            )
            p.wait("gpsimd", "cc", cc_sc)
            dg_sc2 = p.dma_op(
                "gpsimd",
                lambda e: e.dma_start(out=small[0:1, 24:26], in_=sc_out[0:1, 0:2]),
                "scx",
            )
        if t == 10:
            p.wait("tensor", "scx", dg_sc2)
            gate_psum(pD)
            mm_bc = p.op(
                "tensor",
                lambda e: e.matmul(
                    pD[:, 0:2], ones128f[0:1, :], small[0:1, 24:26],
                    start=True, stop=True,
                ),
                inc=True,
            )
            p.wait("vector", "tensor", mm_bc)
            p.op(
                "vector",
                lambda e: e.tensor_scalar(
                    theta[:, 0:2], pD[:, 0:2], float(2.0 ** -25), 0.5e-8,
                    ALU.mult, ALU.max,
                ),
            )
            p.op("vector", lambda e: e.drain())
            v_thn = p.op(
                "vector",
                lambda e: e.tensor_scalar_mul(theta[:, 2:4], theta[:, 0:2], -1.0),
                inc=True,
            )
            psum_free[id(pD)] = ("vector", v_thn)
            emit_quant_units(24)
            p.wait("gpsimd", "qo0", p.dma.get("qo0", 0))
            cc_agw1 = p.cc_op(
                lambda e: e.collective_compute(
                    "AllGather", ALU.bypass,
                    replica_groups=[list(range(NC))],
                    ins=[w1q_m], outs=[w1q_d],
                )
            )
            cc_agw2 = p.cc_op(
                lambda e: e.collective_compute(
                    "AllGather", ALU.bypass,
                    replica_groups=[list(range(NC))],
                    ins=[w2q_m], outs=[w2q_d],
                )
            )

    ln_done_scalar = p.cnt["scalar"]
    p1_pe_end = p.cnt["tensor"]

    if debug_phase >= 1:
        p.wait("gpsimd", "vector", p.cnt["vector"])
        p.wait("gpsimd", "scalar", p.cnt["scalar"])
        p.dma_op("gpsimd", lambda e: e.dma_start(out=dbg["xn_cT"], in_=xn_c[:, :]), "dbg")
        p.dma_op("gpsimd", lambda e: e.dma_start(out=dbg["QT"], in_=QTs[:, :]), "dbg")
        p.dma_op("gpsimd", lambda e: e.dma_start(out=dbg["KT"], in_=KTs[:, :]), "dbg")
        p.dma_op("gpsimd", lambda e: e.dma_start(out=dbg["Vtok"], in_=Vtok[:, :]), "dbg")
    if debug_phase == 1:
        p.wait("gpsimd", "dbg", p.dma["dbg"])
        p.emit(nc)
        return nc

    p.wait("sync", "tensor", p1_pe_end)
    d_masks = p.dma_op(
        "sync",
        lambda e: e.dma_start(out=masks[:, :], in_=masks_h),
        "msk",
    )
    p.op("scalar", lambda e: e.preload_act_table(AF.Exp))
    p.wait("scalar", "tensor", p1_pe_end)
    p.wait("scalar", "qo0", p.dma.get("qo0", 0))
    sq_free = [(None, 0), (None, 0)]
    for h in range(2):
        for which, srcT in (("k", KTs), ("q", QTs)):
            for tq in range(16):
                par = tq % 2
                feng, fcnt = sq_free[par]
                if feng is not None:
                    p.wait("scalar", feng, fcnt)
                ssq = p.op(
                    "scalar",
                    lambda e, h=h, tq=tq, par=par, srcT=srcT: e.square(
                        sq2[:, par * QT : (par + 1) * QT],
                        srcT[:, h * S + tq * 512 : h * S + tq * 512 + 512],
                    ),
                    inc=True,
                )
                ps = pC if par == 0 else pD
                gate_psum(ps)
                p.wait("tensor", "scalar", ssq)
                p.wait("tensor", "gpsimd", ge_ones)
                mm = p.op(
                    "tensor",
                    lambda e, ps=ps, par=par: e.matmul(
                        ps[:, 0:512],
                        ones128[:, :],
                        sq2[:, par * QT : (par + 1) * QT],
                        start=True,
                        stop=True,
                    ),
                    inc=True,
                )
                p.wait("vector", "tensor", mm)
                if which == "q":
                    vr = p.op(
                        "vector",
                        lambda e, ps=ps, h=h, tq=tq: e.reduce_max(
                            qmax[:, h * 16 + tq : h * 16 + tq + 1], ps[:, 0:512], AX.X
                        ),
                        inc=True,
                    )
                    p.op("vector", lambda e: e.drain())
                else:
                    p.op(
                        "vector",
                        lambda e, ps=ps: e.reduce_max(
                            small[:, 28:29], ps[:, 0:512], AX.X
                        ),
                    )
                    p.op("vector", lambda e: e.drain())
                    if tq == 0:
                        vr = p.op(
                            "vector",
                            lambda e, h=h: e.tensor_copy(
                                kmax[:, h : h + 1], small[:, 28:29]
                            ),
                            inc=True,
                        )
                    else:
                        vr = p.op(
                            "vector",
                            lambda e, h=h: e.tensor_max(
                                kmax[:, h : h + 1], kmax[:, h : h + 1], small[:, 28:29]
                            ),
                            inc=True,
                        )
                psum_free[id(ps)] = ("vector", vr)
                sq_free[par] = ("vector", vr)
    v_bias = 0
    for h in range(2):
        p.op(
            "vector",
            lambda e, h=h: e.tensor_scalar_mul(
                small[:, 30:31], kmax[:, h : h + 1], -SCALE / 2.0
            ),
        )
        p.op("vector", lambda e: e.drain())
        for t in range(16):
            v_bias = p.op(
                "vector",
                lambda e, h=h, t=t: e.scalar_tensor_tensor(
                    biasm[:, h * 16 + t : h * 16 + t + 1],
                    qmax[:, h * 16 + t : h * 16 + t + 1],
                    -SCALE / 2.0,
                    small[:, 30:31],
                    ALU.mult,
                    ALU.add,
                ),
                inc=True,
            )

    s_ps = (pC, pD, pE)
    o_ps = {0: pF, 1: pF}
    l_ps = pG
    sp_ps = (pH, pG)
    pT_free = [0, 0, 0]
    sprj_free = [0, 0]
    obar_ready = {}
    mask_gated = [False]
    p.wait("scalar", "vector", v_bias)

    for t in range(NQT):
        nk = min(4 * t + 5, 64)
        for h in range(2):
            mm_l = 0
            av_gate = {}
            for kc in range(nk + 2):
                if kc < nk:
                    par = kc % 3
                    sps = s_ps[par]
                    gate_psum(sps)
                    mm_s = p.op(
                        "tensor",
                        lambda e, sps=sps, h=h, t=t, kc=kc: e.matmul(
                            sps[:, 0:512],
                            KTs[:, h * S + kc * 128 : h * S + kc * 128 + 128],
                            QTs[:, h * S + t * 512 : h * S + t * 512 + 512],
                            start=True,
                            stop=True,
                        ),
                        inc=True,
                    )
                    p.wait("scalar", "tensor", mm_s)
                    p.wait("scalar", "tensor", pT_free[par])
                    c_exp = p.op(
                        "scalar",
                        lambda e, sps=sps, par=par, h=h, t=t: e.activation(
                            pT[:, par * QT : (par + 1) * QT],
                            sps[:, 0:512],
                            AF.Exp,
                            bias=biasm[:, h * 16 + t : h * 16 + t + 1],
                            scale=SCALE,
                        ),
                        inc=True,
                    )
                    psum_free[id(sps)] = ("scalar", c_exp)
                    gate_eng, gate_cnt = "scalar", c_exp
                    m = kc - 4 * t - 1
                    if 0 <= m <= 3:
                        if not mask_gated[0]:
                            p.wait("vector", "msk", d_masks)
                            mask_gated[0] = True
                        p.wait("vector", "scalar", c_exp)
                        c_msk = p.op(
                            "vector",
                            lambda e, par=par, m=m: e.tensor_mul(
                                pT[:, par * QT : (par + 1) * QT],
                                pT[:, par * QT : (par + 1) * QT],
                                masks[:, m * QT : (m + 1) * QT],
                            ),
                            inc=True,
                        )
                        gate_eng, gate_cnt = "vector", c_msk
                    av_gate[kc] = (gate_eng, gate_cnt)
                if kc >= 2:
                    kd = kc - 2
                    par = kd % 3
                    ge_, gc_ = av_gate.pop(kd)
                    p.wait("tensor", ge_, gc_)
                    if kd == 0:
                        gate_psum(o_ps[h])
                        gate_psum(l_ps)
                    p.op(
                        "tensor",
                        lambda e, h=h, kd=kd, par=par, nk=nk: e.matmul(
                            o_ps[h][:, 0:512],
                            Vtok[:, kd * 256 + h * 128 : kd * 256 + h * 128 + 128],
                            pT[:, par * QT : (par + 1) * QT],
                            start=(kd == 0),
                            stop=(kd == nk - 1),
                        ),
                    )
                    mm_l = p.op(
                        "tensor",
                        lambda e, kd=kd, par=par, nk=nk: e.matmul(
                            l_ps[:, 0:512],
                            ones128[:, :],
                            pT[:, par * QT : (par + 1) * QT],
                            start=(kd == 0),
                            stop=(kd == nk - 1),
                        ),
                        inc=True,
                    )
                    pT_free[par] = mm_l
            p.wait("vector", "tensor", mm_l)
            v_ri = p.op(
                "vector", lambda e: e.reciprocal(rinv[:, :], l_ps[:, 0:512]), inc=True
            )
            psum_free[id(l_ps)] = ("vector", v_ri)
            p.op("vector", lambda e: e.drain())
            v_ob = p.op(
                "vector",
                lambda e, h=h: e.tensor_mul(
                    obar[:, h * QT : (h + 1) * QT], o_ps[h][:, 0:512], rinv[:, :]
                ),
                inc=True,
            )
            psum_free[id(o_ps[h])] = ("vector", v_ob)
            obar_ready[h] = v_ob
        p.wait("tensor", "vector", obar_ready[1])
        for dm in range(16):
            par = dm % 2
            sps = sp_ps[par]
            gate_psum(sps)
            for dc in range(2):
                mm_sp = p.op(
                    "tensor",
                    lambda e, sps=sps, dc=dc, dm=dm: e.matmul(
                        sps[:, 0:512],
                        wo_sb[:, (dc * 16 + dm) * 128 : (dc * 16 + dm) * 128 + 128],
                        obar[:, dc * QT : (dc + 1) * QT],
                        start=(dc == 0),
                        stop=(dc == 1),
                    ),
                    inc=(dc == 1),
                )
            p.wait("vector", "tensor", mm_sp)
            p.wait("vector", "sp%d" % par, sprj_free[par])
            v_sp = p.op(
                "vector",
                lambda e, sps=sps, par=par: e.tensor_copy(
                    sprj[:, par * QT : (par + 1) * QT], sps[:, 0:512]
                ),
                inc=True,
            )
            psum_free[id(sps)] = ("vector", v_sp)
            p.wait("gpsimd", "vector", v_sp)
            dg_sp = p.dma_op(
                "gpsimd",
                lambda e, t=t, dm=dm, par=par: e.dma_start(
                    out=s_part[
                        t // 2,
                        dm * 128 : dm * 128 + 128,
                        (t % 2) * 512 : (t % 2) * 512 + 512,
                    ],
                    in_=sprj[:, par * QT : (par + 1) * QT],
                ),
                "sp%d" % par,
            )
            sprj_free[par] = dg_sp
    p.op("scalar", lambda e: e.preload_act_table(AF.Gelu))
    attn_ve_end = p.cnt["vector"]

    if debug_phase >= 4:
        p.wait("gpsimd", "cc", cc_agw1)
        p.wait("gpsimd", "cc", cc_agw2)
        p.dma_op("gpsimd", lambda e: e.dma_start(out=dbg["w1q"], in_=w1q_d[0:8]), "dbg")
        p.dma_op("gpsimd", lambda e: e.dma_start(out=dbg["w2q"], in_=w2q_d), "dbg")
    p.wait("gpsimd", "sp0", p.dma.get("sp0", 0))
    p.wait("gpsimd", "sp1", p.dma.get("sp1", 0))
    cc_rs = p.cc_op(
        lambda e: e.collective_compute(
            "ReduceScatter",
            ALU.add,
            replica_groups=[list(range(NC))],
            ins=[s_part],
            outs=[s_red],
        )
    )

    if debug_phase >= 2:
        p.dma_op(
            "gpsimd",
            lambda e: e.dma_start(
                out=dbg["spart"], in_=s_part.rearrange("a b c -> (a b) c")
            ),
            "dbg",
        )
        p.dma_op("gpsimd", lambda e: e.dma_start(out=dbg["theta"], in_=theta[:, :]), "dbg")
        p.dma_op("gpsimd", lambda e: e.dma_start(out=dbg["biasm"], in_=biasm[:, :]), "dbg")
    if debug_phase == 2:
        p.wait("gpsimd", "cc", cc_rs)
        p.wait("gpsimd", "dbg", p.dma["dbg"])
        p.emit(nc)
        return nc

    h_psums = (pC, pD)
    y_psums = (pE, pF)
    w1s_free = [0, 0]
    w2s_free = [0, 0]
    ysb_free = {}
    silu_done = {}
    ffn_dma_gated = [False]
    silu_gated = [False]
    pe_a_end = [0]
    hdump_gate = [0]
    ydump_gate = [0]
    for qh in range(2):
        gelu_cnt = {}
        if qh == 1 and hdump_gate[0]:
            p.wait("tensor", "dbg", hdump_gate[0])
        if qh == 1 and ydump_gate[0]:
            p.wait("vector", "dbg", ydump_gate[0])
        for fc in range(64):
            par = fc % 2
            if not ffn_dma_gated[0]:
                p.wait("sync", "vector", attn_ve_end)
                p.wait("sync", "sp0", p.dma.get("sp0", 0))
                p.wait("sync", "sp1", p.dma.get("sp1", 0))
                ffn_dma_gated[0] = True
            p.wait("sync", "tensor", w1s_free[par])
            if fc == 0 and qh == 0:
                p.wait("sync", "cc", cc_agw1)
            dv = p.dma_op(
                "sync",
                lambda e, fc=fc, par=par: e.dma_start(
                    out=w1q_sb[:, par * 2048 : (par + 1) * 2048],
                    in_=w1q_d[fc],
                ),
                "w1l%d" % par,
            )
            ps = h_psums[par]
            gate_psum(ps)
            p.wait("tensor", "w1l%d" % par, dv)
            for i in range(16):
                mm = p.op(
                    "tensor",
                    lambda e, ps=ps, par=par, i=i, qh=qh: e.matmul(
                        ps[:, 0:512],
                        w1q_sb[:, par * 2048 + i * 128 : par * 2048 + i * 128 + 128],
                        xn_c[:, i * TS + qh * 512 : i * TS + qh * 512 + 512],
                        start=(i == 0),
                        stop=(i == 15),
                    ),
                    inc=(i == 15),
                )
            w1s_free[par] = mm
            p.wait("scalar", "tensor", mm)
            if not silu_gated[0]:
                p.wait("scalar", "qo0", p.dma.get("qo0", 0))
                silu_gated[0] = True
            c_g = p.op(
                "scalar",
                lambda e, ps=ps, fc=fc: e.activation(
                    h_sb[:, fc * 512 : fc * 512 + 512],
                    ps[:, 0:512],
                    AF.Gelu,
                    bias=b1s[:, fc : fc + 1],
                ),
                inc=True,
            )
            psum_free[id(ps)] = ("scalar", c_g)
            gelu_cnt[fc] = c_g
        pe_a_end[0] = p.cnt["tensor"]
        if debug_phase >= 4 and qh == 0:
            p.wait("gpsimd", "scalar", gelu_cnt[63])
            dg_h0 = p.dma_op(
                "gpsimd", lambda e: e.dma_start(out=dbg["h0"], in_=h_sb[:, :]), "dbg"
            )
            hdump_gate[0] = dg_h0
        for sblk in range(8):
            for dm in range(16):
                par = (sblk * 16 + dm) % 2
                p.wait("sync", "tensor", w2s_free[par])
                if sblk == 0 and dm == 0 and qh == 0:
                    p.wait("sync", "cc", cc_agw2)
                dv = p.dma_op(
                    "sync",
                    lambda e, dm=dm, sblk=sblk, par=par: e.dma_start(
                        out=w2q_sb[:, par * 1024 : (par + 1) * 1024],
                        in_=w2q_d[dm, sblk],
                    ),
                    "w2l%d" % par,
                )
                ps = y_psums[par]
                gate_psum(ps)
                p.wait("tensor", "w2l%d" % par, dv)
                if dm == 0:
                    p.wait("tensor", "scalar", gelu_cnt[sblk * 8 + 7])
                for j in range(8):
                    fc = sblk * 8 + j
                    mm = p.op(
                        "tensor",
                        lambda e, ps=ps, par=par, j=j, fc=fc: e.matmul(
                            ps[:, 0:512],
                            w2q_sb[:, par * 1024 + j * 128 : par * 1024 + j * 128 + 128],
                            h_sb[:, fc * 512 : fc * 512 + 512],
                            start=(j == 0),
                            stop=(j == 7),
                        ),
                        inc=(j == 7),
                    )
                w2s_free[par] = mm
                p.wait("vector", "tensor", mm)
                if sblk == 0 and dm in ysb_free:
                    p.wait("vector", "scalar", ysb_free[dm])
                if sblk == 0:
                    v_f = p.op(
                        "vector",
                        lambda e, ps=ps, dm=dm: e.tensor_copy(
                            y_sb[:, dm * 512 : dm * 512 + 512], ps[:, 0:512]
                        ),
                        inc=True,
                    )
                else:
                    v_f = p.op(
                        "vector",
                        lambda e, ps=ps, dm=dm: e.tensor_add(
                            y_sb[:, dm * 512 : dm * 512 + 512],
                            y_sb[:, dm * 512 : dm * 512 + 512],
                            ps[:, 0:512],
                        ),
                        inc=True,
                    )
                psum_free[id(ps)] = ("vector", v_f)
                if sblk == 7:
                    p.wait("scalar", "vector", v_f)
                    c_s = p.op(
                        "scalar",
                        lambda e, dm=dm, qh=qh: e.activation(
                            g_sb[:, dm * TS + qh * 512 : dm * TS + qh * 512 + 512],
                            y_sb[:, dm * 512 : dm * 512 + 512],
                            AF.Identity,
                            bias=b2s[:, dm : dm + 1],
                        ),
                        inc=True,
                    )
                    ysb_free[dm] = c_s
                    silu_done[(qh, dm)] = c_s
                    if debug_phase >= 4 and qh == 0 and dm == 15:
                        p.wait("gpsimd", "vector", v_f)
                        ydump_gate[0] = p.dma_op(
                            "gpsimd",
                            lambda e: e.dma_start(out=dbg["y0"], in_=y_sb[:, :]),
                            "dbg",
                        )
    pe_ffn_a_end = p.cnt["tensor"]

    if debug_phase >= 4:
        pass
    gdump_dg = 0
    if debug_phase >= 3:
        p.wait("gpsimd", "scalar", max(silu_done.values()))
        gdump_dg = p.dma_op(
            "gpsimd", lambda e: e.dma_start(out=dbg["g"], in_=g_sb[:, :]), "dbg"
        )

    p.op("scalar", lambda e: e.preload_act_table(AF.Silu))
    if gdump_dg:
        p.wait("scalar", "dbg", gdump_dg)
    silu_p5 = {}
    for dm in range(16):
        for half in range(2):
            p.wait("scalar", "scalar", 0)
            silu_p5[(dm, half)] = p.op(
                "scalar",
                lambda e, dm=dm, half=half: e.activation(
                    g_sb[:, dm * TS + half * 512 : dm * TS + half * 512 + 512],
                    g_sb[:, dm * TS + half * 512 : dm * TS + half * 512 + 512],
                    AF.Silu,
                ),
                inc=True,
            )
    p.wait("sync", "cc", cc_rs)
    p.wait("sync", "tensor", pe_ffn_a_end)
    ssb_free = [0, 0]
    osb_free = [0, 0]
    for dm in range(16):
        for half in range(2):
            par = (dm * 2 + half) % 2
            p.wait("sync", "vector", ssb_free[par])
            dv1 = p.dma_op(
                "sync",
                lambda e, dm=dm, half=half, par=par: e.dma_start(
                    out=s_sb[:, par * 512 : (par + 1) * 512],
                    in_=s_red[dm * 128 : dm * 128 + 128, half * 512 : half * 512 + 512],
                ),
                "p5l%d" % par,
            )
            dv2 = p.dma_op(
                "sync",
                lambda e, dm=dm, half=half, par=par: e.dma_start(
                    out=xr_sb[:, par * 512 : (par + 1) * 512],
                    in_=xT_own[dm * 128 : dm * 128 + 128, half * 512 : half * 512 + 512],
                ),
                "p5l%d" % par,
            )
            p.wait("vector", "p5l%d" % par, dv2)
            p.wait("vector", "scalar", silu_p5[(dm, half)])
            p.wait("vector", "ou%d" % par, osb_free[par])
            p.op(
                "vector",
                lambda e, dm=dm, half=half, par=par: e.tensor_mul(
                    o_sb[:, par * 512 : (par + 1) * 512],
                    s_sb[:, par * 512 : (par + 1) * 512],
                    g_sb[:, dm * TS + half * 512 : dm * TS + half * 512 + 512],
                ),
            )
            v_o = p.op(
                "vector",
                lambda e, par=par: e.tensor_add(
                    o_sb[:, par * 512 : (par + 1) * 512],
                    o_sb[:, par * 512 : (par + 1) * 512],
                    xr_sb[:, par * 512 : (par + 1) * 512],
                ),
                inc=True,
            )
            ssb_free[par] = v_o
            p.wait("gpsimd", "vector", v_o)
            osb_free[par] = p.dma_op(
                "gpsimd",
                lambda e, dm=dm, half=half, par=par: e.dma_start(
                    out=yT[dm * 128 : dm * 128 + 128, half * 512 : half * 512 + 512],
                    in_=o_sb[:, par * 512 : (par + 1) * 512],
                ),
                "ou%d" % par,
            )

    p.wait("gpsimd", "ou0", p.dma.get("ou0", 0))
    p.wait("gpsimd", "ou1", p.dma.get("ou1", 0))
    if "dbg" in p.dma:
        p.wait("gpsimd", "dbg", p.dma["dbg"])
    p.emit(nc)
    return nc




def _prep_inputs(L0_attn_x, L0_attn_wq, L0_attn_wk, L0_attn_wv, L0_attn_wo,
                 L0_ffn_w1, L0_ffn_b1, L0_ffn_w2, L0_ffn_b2):
    import ml_dtypes

    bf = ml_dtypes.bfloat16
    f32 = np.float32
    x = np.ascontiguousarray(np.asarray(L0_attn_x, dtype=f32).reshape(S, D))
    xT = np.ascontiguousarray(x.T)
    wq = np.asarray(L0_attn_wq, dtype=f32)
    wk = np.asarray(L0_attn_wk, dtype=f32)
    wv = np.asarray(L0_attn_wv, dtype=f32)
    wo = np.asarray(L0_attn_wo, dtype=f32)
    w1 = np.asarray(L0_ffn_w1, dtype=f32)
    w2 = np.asarray(L0_ffn_w2, dtype=f32)
    b1 = np.asarray(L0_ffn_b1, dtype=f32)
    b2 = np.asarray(L0_ffn_b2, dtype=f32)

    w1T_b = np.ascontiguousarray(
        w1.T.reshape(16, 128, 64, 128).transpose(2, 1, 0, 3).reshape(64, 128, 2048)
    )
    w2T_b = np.ascontiguousarray(
        w2.T.reshape(8, 8, 128, 16, 128).transpose(3, 0, 2, 1, 4).reshape(16, 8, 128, 1024)
    )
    b1_r = np.ascontiguousarray(b1.reshape(64, 128).T)
    b2_r = np.ascontiguousarray(b2.reshape(16, 128).T)
    ident = np.eye(128, dtype=f32).astype(bf)
    ki = np.arange(128)
    qi = np.arange(QT)
    masks = np.concatenate(
        [(qi[None, :] >= ki[:, None] + 128 * m) for m in range(4)], axis=1
    ).astype(bf)

    in_maps = []
    for c in range(NC):
        hs = 256 * c
        in_maps.append(
            {
                "x_tm": x,
                "x_own": np.ascontiguousarray(x[c * TS : (c + 1) * TS, :]),
                "xT_own": np.ascontiguousarray(xT[:, c * TS : (c + 1) * TS]),
                "wq_c": np.ascontiguousarray(
                    wq[hs : hs + 256, :].T.reshape(16, 128, 256).transpose(1, 0, 2)
                ),
                "wk_c": np.ascontiguousarray(
                    wk[hs : hs + 256, :].T.reshape(16, 128, 256).transpose(1, 0, 2)
                ),
                "wv_c": np.ascontiguousarray(
                    wv[hs : hs + 256, :].T.reshape(16, 128, 256).transpose(1, 0, 2)
                ),
                "wo_c": np.ascontiguousarray(
                    wo.T[hs : hs + 256, :]
                    .reshape(2, 128, 16, 128)
                    .transpose(1, 0, 2, 3)
                    .reshape(128, 32, 128)
                ),
                "w1T_b": w1T_b,
                "w2T_b": w2T_b,
                "w1_sl": np.ascontiguousarray(w1T_b[8 * c : 8 * c + 8]),
                "w2_sl": np.ascontiguousarray(
                    w2T_b[2 * c : 2 * c + 2].reshape(16, 128, 1024)
                ),
                "b1_r": b1_r,
                "b2_r": b2_r,
                "ident_h": ident,
                "masks_h": masks,
            }
        )
    return in_maps


def kernel(L0_attn_x, L0_attn_wq, L0_attn_wk, L0_attn_wv, L0_attn_wo,
           L0_ffn_w1, L0_ffn_b1, L0_ffn_w2, L0_ffn_b2):
    in_maps = _prep_inputs(
        L0_attn_x, L0_attn_wq, L0_attn_wk, L0_attn_wv, L0_attn_wo,
        L0_ffn_w1, L0_ffn_b1, L0_ffn_w2, L0_ffn_b2,
    )
    nc = build_program(0)
    res = run_bass_kernel_spmd(nc, in_maps, core_ids=list(range(NC)))
    yT = np.concatenate([r["yT"] for r in res.results], axis=1)
    return np.ascontiguousarray(yT.T).reshape(1, S, D).astype(np.float32)
